# version 05
# baseline (speedup 1.0000x reference)
import math
import numpy as np
import ml_dtypes
from contextlib import ExitStack

import concourse.bass as bass
import concourse.mybir as mybir
from concourse.bass_utils import run_bass_kernel_spmd

F32 = mybir.dt.float32
BF16 = mybir.dt.bfloat16
U8 = mybir.dt.uint8
AF = mybir.ActivationFunctionType
ALU = mybir.AluOpType

B, S, D = 2, 4096, 4096
HEADS, HD, BLK = 32, 128, 256
NCORE = 8
T = (B * S) // NCORE
DC = D // 128
NB = T // BLK
CPS = NCORE // B
EPS = 1e-5
RR = 40
NE = 3
DEBUG = False
ATTN_HEADS = HEADS
USE_CC = True

ENGINES = ("tensor", "vector", "scalar", "sync", "gpsimd")


def _slopes():
    base = 1.0 / 2.0 ** (8.0 / HEADS)
    factor = 1.0 - 1.0 / (HEADS - 1 + 1e-5) + 1e-5
    return np.array([base ** (h + 1.0) * factor for h in range(HEADS)],
                    dtype=np.float64)


SLOPES = _slopes()
BLOCK_DECAY = np.exp(-SLOPES * BLK)


def _panelize(w, o_dim, d_dim):
    oc = o_dim // 128
    t = w.reshape(oc, 128, d_dim // 128, 128).transpose(0, 2, 3, 1)
    t = t.reshape(oc, 2, 16, 128, 128).transpose(0, 1, 3, 2, 4)
    return np.ascontiguousarray(t.reshape(oc * 2, 128, 2048))


def _fourier_m():
    n = D
    t = np.arange(n, dtype=np.float64)
    M = np.empty((n, n), dtype=np.float64)
    M[:, 0] = 1.0 / math.sqrt(n)
    f = np.arange(1, n // 2, dtype=np.float64)
    ang = 2.0 * np.pi * np.outer(t, f) / n
    M[:, 1:n - 1:2] = math.sqrt(2.0 / n) * np.cos(ang)
    M[:, 2:n - 1:2] = math.sqrt(2.0 / n) * np.sin(ang)
    M[:, n - 1] = ((-1.0) ** t) / math.sqrt(n)
    return M


def _decay_tables():
    out = np.zeros((HEADS, 128, 768), dtype=np.float64)
    nidx = np.arange(BLK, dtype=np.float64)
    for h in range(HEADS):
        sl = SLOPES[h]
        for mc in range(2):
            m = (mc * 128 + np.arange(128, dtype=np.float64))[:, None]
            out[h, :, mc * 256:(mc + 1) * 256] = np.where(
                nidx[None, :] >= m, np.exp(-sl * (nidx[None, :] - m)), 0.0)
        out[h, :, 512:768] = np.exp(-sl * (nidx + 1.0))[None, :]
    return out.astype(np.float32)


def _kdecay_table():
    m = np.arange(BLK, dtype=np.float64)[:, None]
    return np.exp(-SLOPES[None, :] * (BLK - 1 - m)).astype(np.float32)


def _prefix_weights(core):
    p = core % CPS
    out = np.zeros((128, HEADS * CPS), dtype=np.float64)
    for h in range(HEADS):
        for r in range(CPS):
            if r < p:
                out[:, h * CPS + r] = BLOCK_DECAY[h] ** (NB * (p - 1 - r))
    return out.astype(np.float32)


class Sched:
    def __init__(self):
        self.ops = {e: [] for e in ENGINES}
        self.count = {}
        self.guar = {e: {} for e in ENGINES}

    def emit(self, eng, fn, waits=(), inc=None, inck=1):
        ws = []
        for (s, v) in waits:
            if v is None or v <= 0:
                continue
            have = self.count.get(s, 0)
            assert v <= have, f"[{eng}] wait {s}>={v} but only {have} emitted"
            if self.guar[eng].get(s, 0) >= v:
                continue
            self.guar[eng][s] = v
            ws.append((s, v))
        newv = None
        if inc is not None:
            self.count[inc] = self.count.get(inc, 0) + inck
            newv = self.count[inc]
            self.guar[eng][inc] = newv
        self.ops[eng].append((ws, fn, (inc, inck) if inc else None))
        return newv

    def tot(self, sem):
        return self.count.get(sem, 0)


class Bank:
    def __init__(self, tile):
        self.tile = tile
        self.deps = []


def build_program(phases=9):
    nc = bass.Bass("TRN2", target_bir_lowering=False)
    sc = Sched()

    def din(name, shape, dtype=F32):
        return nc.dram_tensor(name, list(shape), dtype,
                              kind="ExternalInput").ap()

    xT = din("xT", (D, T))
    qkv_p = din("qkv_p", (192, 128, 2048))
    gate_p = din("gate_p", (64, 128, 2048))
    out_p = din("out_p", (64, 128, 2048))
    wb_p = din("wb_p", (64, 128, 2048))
    mt_p = din("mt_p", (64, 128, 2048), BF16)
    enc01t_in = din("enc01t", (D, RR))
    wcoef = din("wcoef", (D, NE * RR))
    wmask = din("wmask", (D, NE * RR), U8)
    bt_in = din("bt_in", (128, DC))
    nw_in = din("nw_in", (128, DC))
    kdec_in = din("kdec_in", (BLK, HEADS))
    prefw_in = din("prefw_in", (128, 128))
    ident_in = din("ident_in", (128, 128), BF16)
    ones_in = din("ones_in", (128, 2))
    onesb_in = din("onesb_in", (128, 1), BF16)
    onesrow_in = din("onesrow_in", (1, 128))
    dker_in = din("dker_in", (HEADS, 128, 768))

    y_out = nc.dram_tensor("y_out", [D, T], F32, kind="ExternalOutput").ap()
    dbg = {}
    if DEBUG:
        dbg["dbg_x1"] = nc.dram_tensor("dbg_x1", [D, T], F32,
                                       kind="ExternalOutput").ap()
        dbg["dbg_y2"] = nc.dram_tensor("dbg_y2", [D, T], BF16,
                                       kind="ExternalOutput").ap()
        dbg["dbg_x2"] = nc.dram_tensor("dbg_x2", [128, DC * T], F32,
                                       kind="ExternalOutput").ap()
        dbg["dbg_ali"] = nc.dram_tensor("dbg_ali", [128, 65536], BF16,
                                        kind="ExternalOutput").ap()
        dbg["dbg_rstd"] = nc.dram_tensor("dbg_rstd", [1, T], F32,
                                         kind="ExternalOutput").ap()
        dbg["dbg_wef"] = nc.dram_tensor("dbg_wef", [RR, D], BF16,
                                        kind="ExternalOutput").ap()

    x_dram = nc.dram_tensor("x_dram", [128, DC * T], F32)
    cc_in = [nc.dram_tensor(f"cc_in{h}", [128, 128], F32)
             for h in range(HEADS)]
    cc_out = [nc.dram_tensor(f"cc_out{h}", [CPS * 128, 128], F32)
              for h in range(HEADS)]

    with ExitStack() as es:
        e = es.enter_context

        cur = [(nc.sbuf_base + 31) // 32 * 32]

        def alloc(name, shape, dtype):
            esz = {F32: 4, BF16: 2, U8: 1}[dtype]
            off = (cur[0] + 31) // 32 * 32
            cur[0] = off + shape[1] * esz
            return nc.alloc_sbuf_tensor_at(name, list(shape), dtype,
                                           offset=off)

        ident = alloc("ident", (128, 128), BF16)
        onesb = alloc("onesb", (128, 1), BF16)
        ones32t = alloc("ones32", (128, 2), F32)
        onesrow = alloc("onesrow", (1, 128), F32)
        b_half = alloc("b_half", (128, DC), F32)
        nw = alloc("nw", (128, DC), F32)
        kdecst = alloc("kdecst", (128, 2 * HEADS), F32)
        prefw = alloc("prefw", (128, 128), F32)
        enc01T = alloc("enc01T", (128, DC * RR), BF16)
        m_sb = alloc("m_sb", (1, 512), F32)
        rstd_sb = alloc("rstd_sb", (1, 512), F32)
        sumsq_row = alloc("sumsq_row", (1, T), F32)
        rstd_row = alloc("rstd_row", (1, T), F32)
        bc0 = alloc("bc0", (128, 512), F32)
        bc1 = alloc("bc1", (128, 512), F32)
        e01_sb = alloc("e01_sb", (RR, 512), BF16)
        sqscr = alloc("sqscr", (128, 512), BF16)
        xbscr = alloc("xbscr", (128, 512), BF16)
        att2 = alloc("att2", (128, 256), BF16)
        wp = [alloc("wp0", (128, 2048), BF16), alloc("wp1", (128, 2048), BF16)]
        t_ev = [alloc("t_ev0", (128, 512), F32),
                alloc("t_ev1", (128, 512), F32)]
        weff01T = alloc("weff01T", (RR, D), BF16)
        yf32 = alloc("yf32", (128, 512), F32)

        offA = (cur[0] + 31) // 32 * 32
        SZA = 32768

        def alloc_at(name, shape, dtype, off):
            esz = {F32: 4, BF16: 2, U8: 1}[dtype]
            assert off % 32 == 0 and off + shape[1] * esz <= offA + SZA, name
            return nc.alloc_sbuf_tensor_at(name, list(shape), dtype,
                                           offset=off)

        y1h = alloc_at("y1h", (128, DC * 512), BF16, offA)
        w01sum = alloc_at("w01sum", (128, DC * RR), BF16, offA)
        coefscr = alloc_at("coefscr", (128, NE * RR), F32, offA + 2592)
        maskscr = alloc_at("maskscr", (128, NE * RR), U8, offA + 3104)
        wef_s = alloc_at("wef_s", (128, RR), BF16, offA + 3232)
        bcA2 = alloc_at("bcA2", (128, 512), F32, offA)
        bcB2 = alloc_at("bcB2", (128, 512), F32, offA + 2048)
        m2_sb = alloc_at("m2_sb", (1, 512), F32, offA + 4096)
        r2_sb = alloc_at("r2_sb", (1, 512), F32, offA + 6144)
        qkvA = alloc_at("qkvA", (128, 3 * T), BF16, offA)
        qkvB = alloc_at("qkvB", (128, 3 * T), BF16, offA + 6144)
        gatenw = alloc_at("gatenw", (128, T), F32, offA + 12288)
        vT = alloc_at("vT", (128, NB * 256), BF16, offA + 16384)
        ktT = alloc_at("ktT", (128, 256), BF16, offA + 18432)
        sdecT = alloc_at("sdecT", (128, 512), BF16, offA + 18944)
        qdecs = alloc_at("qdecs", (128, 256), BF16, offA + 19968)
        dker = [alloc_at("dker0", (128, 768), F32, offA + 20480),
                alloc_at("dker1", (128, 768), F32, offA + 23552)]
        kvf = alloc_at("kvf", (128, 128), F32, offA + 26624)
        kvb = alloc_at("kvb", (128, 128), BF16, offA + 27136)
        Ah = alloc_at("Ah", (128, 128), F32, offA + 27392)
        Agath = alloc_at("Agath", (128, CPS * 128), F32, offA + 27904)
        Cst = alloc_at("Cst", (128, NB * 128), F32, offA + 29952)
        kdecT = alloc_at("kdecT", (128, 256), BF16, offA + 32000)
        sc128 = alloc_at("sc128", (128, 128), F32, offA + 32000)

        offX = offA + SZA
        x_ar = nc.alloc_sbuf_tensor_at("x_ar", [128, DC * T], F32,
                                       offset=offX)
        xali = nc.alloc_sbuf_tensor_at("xali", [128, 65536], BF16,
                                       offset=offX)
        xh = nc.alloc_sbuf_tensor_at("xh", [128, DC * 512], F32, offset=offX)
        y3h = nc.alloc_sbuf_tensor_at("y3h", [128, DC * 512], BF16,
                                      offset=offX + 65536)
        assert offX + 131072 <= nc.sbuf_top, \
            f"SBUF overflow: end {offX + 131072} > top {nc.sbuf_top}"

        def y2_t(dc):
            return xali[:, dc * 2048: dc * 2048 + 1024]

        def attg_t(h):
            return xali[:, h * 2048 + 1024: (h + 1) * 2048]

        def x_t(dc):
            return x_ar[:, dc * T: (dc + 1) * T]

        pmain = [e(nc.psum_tensor(f"p{i}", [128, 512], F32))
                 for i in range(4)]
        pT = e(nc.psum_tensor("pT", [128, 1024], BF16))
        pE = e(nc.psum_tensor("pE", [128, 512], F32))
        pR1 = e(nc.psum_tensor("pR1", [1, 512], F32))
        pR2 = e(nc.psum_tensor("pR2", [1, 512], F32))
        banks = [Bank(t) for t in pmain] + [Bank(pT), Bank(pE), Bank(pR1),
                                            Bank(pR2)]
        BK_T, BK_E, BK_R1, BK_R2 = 4, 5, 6, 7

        sems = {}
        semnames = ["dma_k", "cofS", "xldS", "spillS", "wpS0", "wpS1", "dkS0",
                    "dkS1", "csS", "agS", "trS", "trC", "accS0", "accS1",
                    "xhS", "outS", "gkS", "pe", "dve", "act"]
        for nm in semnames:
            sems[nm] = e(nc.semaphore(nm))
        for h in range(HEADS):
            nm = f"cc{h}"
            sems[nm] = e(nc.semaphore(nm))
            semnames.append(nm)

        def E(eng, fn, waits=(), inc=None, inck=1):
            return sc.emit(eng, fn, waits, inc, inck)

        def dma(q, sem, out_ap, in_ap, waits=(), **kw):
            return E(q, lambda g: g.dma_start(out=out_ap, in_=in_ap, **kw),
                     waits, inc=sem, inck=16)

        def mm(bank_i, out_ap, lhsT, rhs, start, stop, waits=(), inc=False):
            b = banks[bank_i]
            w = list(waits) + (b.deps if start else [])
            if start:
                b.deps = []
            return E("tensor",
                     lambda t: t.matmul(out_ap, lhsT, rhs, start=start,
                                        stop=stop),
                     w, inc="pe" if (stop or inc) else None)

        def bank_done(bank_i, sem, val):
            banks[bank_i].deps = [(sem, val)]

        def bank_add(bank_i, sem, val):
            banks[bank_i].deps.append((sem, val))

        st = {"wp_free": [[], []], "wp_inst": [0, 0],
              "tev_free": [[], []], "acc_inst": [0, 0]}

        def load_panel(pi, src_ap, cast):
            st["wp_inst"][pi] += 1
            sem = f"wpS{pi}"
            dma("gpsimd" if cast else "sync", sem, wp[pi][:, :], src_ap,
                waits=st["wp_free"][pi])
            return (sem, 16 * st["wp_inst"][pi])

        dma("sync", "dma_k", ident[:, :], ident_in)
        dma("sync", "dma_k", b_half[:, :], bt_in)
        dma("sync", "dma_k", nw[:, :], nw_in)
        dma("sync", "dma_k", kdecst[:, 0:HEADS], kdec_in[0:128, :])
        dma("sync", "dma_k", kdecst[:, HEADS:2 * HEADS], kdec_in[128:256, :])
        dma("sync", "dma_k", prefw[:, :], prefw_in)
        dma("sync", "dma_k", ones32t[:, :], ones_in)
        dma("sync", "dma_k", onesb[:, :], onesb_in)
        dma("sync", "dma_k", onesrow[:, :], onesrow_in)
        K_CONST = sc.tot("dma_k")
        gk = dma("gpsimd", "gkS",
                 enc01T.ap().rearrange("p (c r) -> p c r", r=RR),
                 enc01t_in.rearrange("(c p) r -> p c r", p=128))
        c_bh = E("vector",
                 lambda v: v.tensor_scalar(b_half[:, :], b_half[:, :], 0.5,
                                           None, ALU.mult),
                 [("dma_k", K_CONST)], inc="dve")

        w01_done = 0
        for dc in range(DC):
            dma("sync", "cofS", coefscr[:, :],
                wcoef[dc * 128:(dc + 1) * 128, :], waits=[("dve", w01_done)])
            dma("sync", "cofS", maskscr[:, :],
                wmask[dc * 128:(dc + 1) * 128, :], waits=[("dve", w01_done)])
            kc = sc.tot("cofS")
            E("vector",
              lambda v: v.tensor_tensor(coefscr[:, :], coefscr[:, :],
                                        maskscr[:, :], ALU.mult),
              [("cofS", kc)], inc="dve")
            E("vector",
              lambda v: v.tensor_tensor(coefscr[:, 0:RR], coefscr[:, 0:RR],
                                        coefscr[:, RR:2 * RR], ALU.add),
              inc="dve")
            w01_done = E("vector",
                         lambda v, d=dc: v.tensor_tensor(
                             w01sum[:, d * RR:(d + 1) * RR],
                             coefscr[:, 0:RR], coefscr[:, 2 * RR:3 * RR],
                             ALU.add),
                         inc="dve")

        weff_done = 0
        for oc in range(DC):
            pv = 0
            for hp in range(2):
                pw = load_panel(hp, mt_p[2 * oc + hp], cast=False)
                for k in range(16):
                    dcc = hp * 16 + k
                    pv = mm(BK_E, pE[:, 0:RR],
                            wp[hp][:, k * 128:(k + 1) * 128],
                            w01sum[:, dcc * RR:(dcc + 1) * RR],
                            start=(dcc == 0), stop=(dcc == DC - 1),
                            waits=[pw, ("dve", w01_done)], inc=(k == 15))
                st["wp_free"][hp] = [("pe", pv)]
            ev = E("vector",
                   lambda v: v.tensor_scalar(wef_s[:, :], pE[:, 0:RR], 0.5,
                                             None, ALU.mult),
                   [("pe", pv)], inc="dve")
            bank_done(BK_E, "dve", ev)
            tv = E("tensor",
                   lambda t: t.transpose(pT[0:RR, 0:128], wef_s[:, :],
                                         ident[:, :]),
                   [("dve", ev), ("dma_k", K_CONST)] + banks[BK_T].deps,
                   inc="pe")
            banks[BK_T].deps = []
            weff_done = E("vector",
                          lambda v, o=oc: v.tensor_copy(
                              weff01T[:, o * 128:(o + 1) * 128],
                              pT[0:RR, 0:128]),
                          [("pe", tv)], inc="dve")
            bank_done(BK_T, "dve", weff_done)
        if DEBUG:
            dma("sync", "outS", dbg["dbg_wef"], weff01T[:, :],
                [("dve", weff_done)])

        for i in range(8):
            dma("sync", "xldS",
                x_ar.ap().rearrange("p (c n) -> p c n", n=T)[
                    :, i * 4:(i + 1) * 4, :],
                xT.rearrange("(c p) n -> p c n", p=128)[
                    :, i * 4:(i + 1) * 4, :])
        XLD = sc.tot("xldS")

        def _finish():
            for i in range(8):
                dma("sync", "outS",
                    y_out.rearrange("(c p) n -> p c n", p=128)[
                        :, i * 4:(i + 1) * 4, :],
                    x_ar.ap().rearrange("p (c n) -> p c n", n=T)[
                        :, i * 4:(i + 1) * 4, :],
                    [("dve", sc.tot("dve")), ("act", sc.tot("act")),
                     ("pe", sc.tot("pe")), ("xldS", XLD)])
            E("gpsimd", lambda g: None,
              [(nm, sc.tot(nm)) for nm in semnames if sc.tot(nm) > 0])
            block = e(nc.Block())
            for eng_name in ENGINES:
                ops = sc.ops[eng_name]
                if not ops:
                    continue

                def body(engo, _ops=ops):
                    for (ws, fn, inc) in _ops:
                        for (sname, v) in ws:
                            engo.wait_ge(sems[sname], v)
                        instr = fn(engo)
                        if inc is not None and instr is not None:
                            instr.then_inc(sems[inc[0]], inc[1])

                getattr(block, eng_name)(body)

        if phases <= 0:
            _finish()
            return nc

        def ln_stats(src, col0, m_dst, r_dst, extra_waits):
            r1v, r2v = 0, 0
            for dc in range(DC):
                w = list(extra_waits(dc))
                cv = E("vector",
                       lambda v, ap=src(dc): v.tensor_copy(
                           xbscr[:, :], ap[:, col0:col0 + 512]),
                       w + [("pe", r1v)], inc="dve")
                sv = E("scalar",
                       lambda s, ap=src(dc): s.activation(
                           sqscr[:, :], ap[:, col0:col0 + 512], AF.Square),
                       w + [("pe", r2v)], inc="act")
                r1v = mm(BK_R1, pR1[:, :], onesb[:, :], xbscr[:, :],
                         start=(dc == 0), stop=(dc == DC - 1),
                         waits=[("dve", cv), ("dma_k", K_CONST)], inc=True)
                r2v = mm(BK_R2, pR2[:, :], onesb[:, :], sqscr[:, :],
                         start=(dc == 0), stop=(dc == DC - 1),
                         waits=[("act", sv)], inc=True)
            mv = E("vector",
                   lambda v: v.tensor_scalar(m_dst[:, :], pR1[:, :],
                                             1.0 / D, None, ALU.mult),
                   [("pe", r2v)], inc="dve")
            E("vector",
              lambda v: v.tensor_scalar(r_dst[:, :], pR2[:, :], 1.0 / D,
                                        EPS, ALU.mult, ALU.add),
              inc="dve")
            E("vector",
              lambda v: v.tensor_tensor(pR1[0:1, :], m_dst[:, :],
                                        m_dst[:, :], ALU.mult),
              inc="dve")
            vv = E("vector",
                   lambda v: v.tensor_tensor(r_dst[:, :], r_dst[:, :],
                                             pR1[0:1, :], ALU.subtract),
                   inc="dve")
            bank_done(BK_R1, "dve", vv)
            sv = E("scalar",
                   lambda s: s.activation(r_dst[:, :], r_dst[:, :], AF.Sqrt),
                   [("dve", vv)], inc="act")
            rv = E("vector", lambda v: v.reciprocal(r_dst[:, :], r_dst[:, :]),
                   [("act", sv)], inc="dve")
            bank_done(BK_R2, "dve", rv)
            return rv

        def bcast(row_ap, dst, waits):
            w = list(waits) + banks[BK_E].deps + [("dma_k", K_CONST)]
            banks[BK_E].deps = []
            pv = E("tensor",
                   lambda t: t.matmul(pE[:, :], onesrow[:, :], row_ap,
                                      start=True, stop=True),
                   w, inc="pe")
            cv = E("vector", lambda v: v.tensor_copy(dst[:, :], pE[:, :]),
                   [("pe", pv)], inc="dve")
            bank_done(BK_E, "dve", cv)
            return cv

        def e01_calc(ysrc, ywait):
            pv = 0
            for dc in range(DC):
                pv = mm(BK_E, pE[0:RR, :],
                        enc01T[:, dc * RR:(dc + 1) * RR], ysrc(dc),
                        start=(dc == 0), stop=(dc == DC - 1),
                        waits=[("dve", ywait), ("gkS", gk)])
            ev = E("vector",
                   lambda v: v.tensor_copy(e01_sb[:, :], pE[0:RR, :]),
                   [("pe", pv)], inc="dve")
            bank_done(BK_E, "dve", ev)
            return ev

        def ffn_half(src, src_col0, ydst, out_cb, stats_extra):
            pe_entry = sc.tot("pe")
            rv = ln_stats(src, src_col0, m_sb, rstd_sb, stats_extra)
            b0 = bcast(rstd_sb[:, :], bc0, [("dve", rv)])
            b1 = bcast(m_sb[:, :], bc1, [("dve", rv)])
            ylast = 0
            for dc in range(DC):
                E("vector",
                  lambda v, sa=src(dc): v.tensor_tensor(
                      yf32[:, :], sa[:, src_col0:src_col0 + 512], bc1[:, :],
                      ALU.subtract),
                  [("dve", b1), ("pe", pe_entry)], inc="dve")
                ylast = E("vector",
                          lambda v, yd=ydst(dc): v.tensor_tensor(
                              yd, yf32[:, :], bc0[:, :], ALU.mult),
                          [("dve", b0)], inc="dve")
            ev01 = e01_calc(ydst, ylast)
            for oc in range(DC):
                bi = oc % 4
                for hp in range(2):
                    pw = load_panel(hp, wb_p[2 * oc + hp], cast=True)
                    pvp = 0
                    for k in range(16):
                        dcc = hp * 16 + k
                        pvp = mm(bi, pmain[bi][:, :],
                                 wp[hp][:, k * 128:(k + 1) * 128], ydst(dcc),
                                 start=(dcc == 0), stop=False,
                                 waits=[pw, ("dve", ylast)], inc=(k == 15))
                    st["wp_free"][hp] = [("pe", pvp)]
                pv = mm(bi, pmain[bi][:, :],
                        weff01T[:, oc * 128:(oc + 1) * 128], e01_sb[:, :],
                        start=False, stop=True,
                        waits=[("dve", ev01), ("dve", weff_done)])
                out_cb(oc, pv, bi)

        def make_ffn1_out(half):
            def out_cb(oc, pv, bi):
                ti = oc % 2
                av = E("scalar",
                       lambda s, o=oc, b=bi, t=ti: s.activation(
                           t_ev[t][:, :], pmain[b][:, :], AF.Identity,
                           bias=b_half[:, o:o + 1], scale=0.5),
                       [("pe", pv), ("dve", c_bh)] + st["tev_free"][ti],
                       inc="act")
                bank_done(bi, "act", av)
                xv = E("vector",
                       lambda v, o=oc, t=ti, h=half: v.tensor_tensor(
                           x_t(o)[:, h * 512:(h + 1) * 512],
                           x_t(o)[:, h * 512:(h + 1) * 512],
                           t_ev[t][:, :], ALU.add),
                       [("act", av)], inc="dve")
                st["tev_free"][ti] = [("dve", xv)]
            return out_cb

        for half in range(2):
            ffn_half(x_t, half * 512,
                     lambda dc: y1h[:, dc * 512:(dc + 1) * 512],
                     make_ffn1_out(half),
                     stats_extra=lambda dc: [("xldS", XLD),
                                             ("dve", weff_done)])
        ffn1_done = sc.tot("dve")
        if DEBUG:
            for i in range(8):
                dma("sync", "outS",
                    dbg["dbg_x1"].rearrange("(c p) n -> p c n", p=128)[
                        :, i * 4:(i + 1) * 4, :],
                    x_ar.ap().rearrange("p (c n) -> p c n", n=T)[
                        :, i * 4:(i + 1) * 4, :],
                    [("dve", ffn1_done)])
        if phases <= 1:
            _finish()
            return nc

        rv0 = ln_stats(x_t, 0, m_sb, rstd_sb, lambda dc: [("dve", ffn1_done)])
        rv1 = ln_stats(x_t, 512, m2_sb, r2_sb, lambda dc: [])
        for dc in range(DC):
            dma("sync", "spillS", x_dram.ap()[:, dc * T:(dc + 1) * T],
                x_t(dc), [("dve", ffn1_done)])
        SPILL = sc.tot("spillS")
        bB0 = bcast(m_sb[:, :], bc1, [("dve", rv0)])
        bA0 = bcast(rstd_sb[:, :], bc0, [("dve", rv0)])
        bA1 = bcast(r2_sb[:, :], bcA2, [("dve", rv1)])
        bB1 = bcast(m2_sb[:, :], bcB2, [("dve", rv1)])
        y2_done = 0
        for dc in range(DC):
            for half, bR, bM, wv in ((0, bc0, bc1, bA0), (1, bcA2, bcB2, bB1)):
                E("vector",
                  lambda v, d=dc, h=half, m=bM: v.tensor_tensor(
                      yf32[:, :],
                      x_t(d)[:, h * 512:(h + 1) * 512], m[:, :],
                      ALU.subtract),
                  [("dve", wv), ("dve", bB1)], inc="dve")
                y2_done = E("vector",
                            lambda v, d=dc, h=half, r=bR: v.tensor_tensor(
                                y2_t(d)[:, h * 512:(h + 1) * 512],
                                yf32[:, :], r[:, :], ALU.mult),
                            [("spillS", SPILL)], inc="dve")
        if DEBUG:
            for dc in range(DC):
                dma("sync", "outS", dbg["dbg_y2"][dc * 128:(dc + 1) * 128, :],
                    y2_t(dc), [("dve", y2_done)])
        if phases <= 2:
            _finish()
            return nc

        qkv_store = [qkvA, qkvB]
        qs_free = [[("dve", y2_done)], [("dve", y2_done)]]
        dk_inst = [0, 0]
        dk_free = [[("dve", y2_done)], [("dve", y2_done)]]
        ah_free = [("dve", y2_done)]
        gn_free = [("dve", y2_done)]
        ag_free = [("dve", y2_done)]
        tr_gate = [("dve", y2_done), ("spillS", SPILL)]
        sq_init = True

        for h in range(ATTN_HEADS):
            si = h % 2
            store = qkv_store[si]
            dki = h % 2
            bd = float(BLOCK_DECAY[h])
            dk_inst[dki] += 1
            dma("scalar", f"dkS{dki}", dker[dki][:, :], dker_in[h],
                waits=dk_free[dki])
            cdk_w = (f"dkS{dki}", 16 * dk_inst[dki])
            qkv_done = 0
            for ot3 in range(3):
                ptile = 3 * h + ot3
                bi0, bi1 = (2 * ot3) % 4, (2 * ot3 + 1) % 4
                pv = 0
                for hp in range(2):
                    pw = load_panel(hp, qkv_p[2 * ptile + hp], cast=True)
                    for k in range(16):
                        dcc = hp * 16 + k
                        for half, bi in ((0, bi0), (1, bi1)):
                            pv = mm(bi, pmain[bi][:, :],
                                    wp[hp][:, k * 128:(k + 1) * 128],
                                    y2_t(dcc)[:, half * 512:(half + 1) * 512],
                                    start=(dcc == 0), stop=(dcc == DC - 1),
                                    waits=[pw, ("dve", y2_done)] +
                                          (qs_free[si] if (dcc == 0 and
                                           ot3 == 0 and half == 0) else []),
                                    inc=(k == 15 and half == 1))
                    st["wp_free"][hp] = [("pe", pv)]
                for half, bi in ((0, bi0), (1, bi1)):
                    qkv_done = E(
                        "scalar",
                        lambda s, b=bi,
                        ap=store[:, ot3 * T + half * 512:
                                 ot3 * T + (half + 1) * 512]:
                        s.activation(ap, pmain[b][:, :], AF.Silu),
                        [("pe", pv)], inc="act")
                    bank_done(bi, "act", qkv_done)
            cmm_prev = 0
            avm = 0
            for b in range(NB):
                for mc in range(2):
                    dma("sync", "trS", ktT[:, mc * 128:(mc + 1) * 128],
                        store[:, T + b * 256 + mc * 128:
                              T + b * 256 + (mc + 1) * 128],
                        tr_gate + [("act", qkv_done), ("pe", cmm_prev)],
                        transpose=True)
                    dma("sync", "trS", vT[:, b * 256 + mc * 128:
                                          b * 256 + (mc + 1) * 128],
                        store[:, 2 * T + b * 256 + mc * 128:
                              2 * T + b * 256 + (mc + 1) * 128],
                        tr_gate + [("act", qkv_done), ("pe", cmm_prev)],
                        transpose=True)
                trk_s = trk_c = sc.tot("trS")
                kv2 = 0
                for mc in range(2):
                    kv2 = E("vector",
                            lambda v, m=mc, hh=h: v.tensor_scalar(
                                kdecT[:, m * 128:(m + 1) * 128],
                                ktT[:, m * 128:(m + 1) * 128],
                                kdecst[:, m * HEADS + hh:m * HEADS + hh + 1],
                                None, ALU.mult),
                            [("trS", trk_s), ("dma_k", K_CONST)], inc="dve")
                pv = 0
                for mc in range(2):
                    pv = mm(BK_E, pE[:, 0:128],
                            kdecT[:, mc * 128:(mc + 1) * 128],
                            vT[:, b * 256 + mc * 128:b * 256 + (mc + 1) * 128],
                            start=(mc == 0), stop=(mc == 1),
                            waits=[("dve", kv2), ("trS", trk_c)])
                cmm_prev = pv
                cvv = E("vector",
                        lambda v, b_=b: v.tensor_copy(
                            Cst[:, b_ * 128:(b_ + 1) * 128], pE[:, 0:128]),
                        [("pe", pv)], inc="dve")
                bank_done(BK_E, "dve", cvv)
                if b == 0:
                    avm = E("vector",
                            lambda v: v.tensor_copy(Ah[:, :], Cst[:, 0:128]),
                            ah_free, inc="dve")
                else:
                    E("vector",
                      lambda v, d=bd: v.tensor_scalar(Ah[:, :], Ah[:, :], d,
                                                      None, ALU.mult),
                      inc="dve")
                    avm = E("vector",
                            lambda v, b_=b: v.tensor_tensor(
                                Ah[:, :], Ah[:, :],
                                Cst[:, b_ * 128:(b_ + 1) * 128], ALU.add),
                            inc="dve")
            cs = dma("scalar", "csS", cc_in[h].ap(), Ah[:, :],
                     [("dve", avm)])
            ah_free = [("csS", cs)]
            if USE_CC:
                E("gpsimd",
                  lambda g, h_=h: g.collective_compute(
                      "AllGather", ALU.bypass,
                      replica_groups=[[0, 1, 2, 3], [4, 5, 6, 7]],
                      ins=[cc_in[h_].ap().opt()],
                      outs=[cc_out[h_].ap().opt()]),
                  [("csS", cs)], inc=f"cc{h}")
            else:
                for r in range(CPS):
                    E("gpsimd",
                      lambda g, h_=h, r_=r: g.dma_start(
                          out=cc_out[h_].ap()[r_ * 128:(r_ + 1) * 128, :],
                          in_=cc_in[h_].ap()),
                      [("csS", cs)], inc=f"cc{h}", inck=16)
            pv = 0
            for hp in range(2):
                pw = load_panel(hp, gate_p[2 * h + hp], cast=True)
                for k in range(16):
                    dcc = hp * 16 + k
                    for half, bi in ((0, 0), (1, 1)):
                        pv = mm(bi, pmain[bi][:, :],
                                wp[hp][:, k * 128:(k + 1) * 128],
                                y2_t(dcc)[:, half * 512:(half + 1) * 512],
                                start=(dcc == 0), stop=(dcc == DC - 1),
                                waits=[pw], inc=(k == 15 and half == 1))
                st["wp_free"][hp] = [("pe", pv)]
            gv = 0
            for half, bi in ((0, 0), (1, 1)):
                gv = E("scalar",
                       lambda s, b=bi, hf=half: s.activation(
                           gatenw[:, hf * 512:(hf + 1) * 512],
                           pmain[b][:, :], AF.Sigmoid),
                       [("pe", pv)] + gn_free, inc="act")
                bank_done(bi, "act", gv)
            gnv = E("vector",
                    lambda v, hh=h: v.tensor_scalar(
                        gatenw[:, :], gatenw[:, :], nw[:, hh:hh + 1],
                        None, ALU.mult),
                    [("act", gv), ("dma_k", K_CONST)], inc="dve")
            ag = dma("scalar", "agS", Agath[:, :],
                     cc_out[h].ap().rearrange("(r p) e -> p r e", p=128),
                     [(f"cc{h}", 1)] + ag_free)
            E("vector",
              lambda v, hh=h: v.tensor_scalar(
                  kvf[:, :], Agath[:, 0:128],
                  prefw[:, hh * CPS:hh * CPS + 1], None, ALU.mult),
              [("agS", ag), ("dma_k", K_CONST)], inc="dve")
            for r in range(1, CPS):
                E("vector",
                  lambda v, r_=r, hh=h: v.tensor_scalar(
                      sc128[:, :], Agath[:, r_ * 128:(r_ + 1) * 128],
                      prefw[:, hh * CPS + r_:hh * CPS + r_ + 1],
                      None, ALU.mult),
                  inc="dve")
                E("vector",
                  lambda v: v.tensor_tensor(kvf[:, :], kvf[:, :],
                                            sc128[:, :], ALU.add),
                  inc="dve")
            kbv = E("vector", lambda v: v.tensor_copy(kvb[:, :], kvf[:, :]),
                    inc="dve")
            last_attg = 0
            sq_free = []
            for b in range(NB):
                spv = []
                for mc in range(2):
                    spv.append(mm(
                        mc, pmain[mc][:, 0:256],
                        store[:, T + b * 256 + mc * 128:
                              T + b * 256 + (mc + 1) * 128],
                        store[:, b * 256:(b + 1) * 256],
                        start=True, stop=True, waits=[("act", qkv_done)]))
                sdv = []
                for mc in range(2):
                    dv = E("vector",
                           lambda v, m=mc, dk=dki: v.tensor_tensor(
                               sdecT[:, m * 256:(m + 1) * 256],
                               pmain[m][:, 0:256],
                               dker[dk][:, m * 256:(m + 1) * 256], ALU.mult),
                           [("pe", spv[mc]), cdk_w] + sq_free, inc="dve")
                    bank_done(mc, "dve", dv)
                    sdv.append(dv)
                qv = E("vector",
                       lambda v, ap=store[:, b * 256:(b + 1) * 256],
                       dk=dki: v.tensor_tensor(
                           qdecs[:, :], ap, dker[dk][:, 512:768], ALU.mult),
                       [("act", qkv_done), cdk_w] + sq_free, inc="dve")
                abi = 2 + (b % 2)
                for mc in range(2):
                    mm(abi, pmain[abi][:, 0:256],
                       vT[:, b * 256 + mc * 128:b * 256 + (mc + 1) * 128],
                       sdecT[:, mc * 256:(mc + 1) * 256],
                       start=(mc == 0), stop=False,
                       waits=[("dve", sdv[mc])])
                apv = mm(abi, pmain[abi][:, 0:256], kvb[:, :],
                         qdecs[:, :], start=False, stop=True,
                         waits=[("dve", qv), ("dve", kbv)])
                if b < NB - 1:
                    E("vector",
                      lambda v, d=bd: v.tensor_scalar(
                          kvf[:, :], kvf[:, :], d, None, ALU.mult),
                      [("pe", apv)], inc="dve")
                    E("vector",
                      lambda v, b_=b: v.tensor_tensor(
                          kvf[:, :], kvf[:, :],
                          Cst[:, b_ * 128:(b_ + 1) * 128], ALU.add),
                      inc="dve")
                    kbv = E("vector",
                            lambda v: v.tensor_copy(kvb[:, :], kvf[:, :]),
                            inc="dve")
                last_attg = E("vector",
                              lambda v, a0=pmain[abi][:, 0:256],
                              a1=gatenw[:, b * 256:(b + 1) * 256],
                              ao=attg_t(h)[:, b * 256:(b + 1) * 256]:
                              v.tensor_tensor(ao, a0, a1, ALU.mult),
                              [("dve", gnv), ("pe", apv)] +
                              ([("spillS", SPILL)] if (h == 0 and b == 0)
                               else []),
                              inc="dve")
                sqv = E("scalar",
                        lambda s, a=abi: s.activation(
                            att2[:, :], pmain[a][:, 0:256], AF.Square),
                        [("pe", apv), ("dve", last_attg)] + sq_free,
                        inc="act")
                bank_done(abi, "dve", last_attg)
                bank_add(abi, "act", sqv)
                w = banks[BK_R1].deps
                banks[BK_R1].deps = []
                spq = E("tensor",
                        lambda t: t.matmul(pR1[0:1, 0:256], onesb[:, :],
                                           att2[:, :], start=True,
                                           stop=True),
                        [("act", sqv)] + w, inc="pe")
                if sq_init:
                    sqa = E("vector",
                            lambda v, b_=b: v.tensor_copy(
                                sumsq_row[:, b_ * 256:(b_ + 1) * 256],
                                pR1[0:1, 0:256]),
                            [("pe", spq)], inc="dve")
                else:
                    sqa = E("vector",
                            lambda v, b_=b: v.tensor_tensor(
                                sumsq_row[:, b_ * 256:(b_ + 1) * 256],
                                sumsq_row[:, b_ * 256:(b_ + 1) * 256],
                                pR1[0:1, 0:256], ALU.add),
                            [("pe", spq)], inc="dve")
                bank_done(BK_R1, "dve", sqa)
                sq_free = [("dve", sqa)]
            sq_init = False
            qs_free[si] = [("dve", sc.tot("dve"))]
            dk_free[dki] = [("dve", sc.tot("dve"))]
            gn_free = [("dve", last_attg)]
            ag_free = [("dve", kbv)]
            tr_gate = [("pe", sc.tot("pe"))]
        attn_done = sc.tot("dve")
        if DEBUG:
            dma("sync", "outS", dbg["dbg_ali"], xali[:, :],
                [("dve", attn_done)])
        if phases <= 3:
            _finish()
            return nc

        tsv = E("vector",
                lambda v: v.tensor_scalar(rstd_row[:, :], sumsq_row[:, :],
                                          1.0 / D, EPS, ALU.mult, ALU.add),
                [("dve", attn_done)], inc="dve")
        ssv = E("scalar",
                lambda s: s.activation(rstd_row[:, :], rstd_row[:, :],
                                       AF.Sqrt),
                [("dve", tsv)], inc="act")
        rrv = E("vector",
                lambda v: v.reciprocal(rstd_row[:, :], rstd_row[:, :]),
                [("act", ssv)], inc="dve")
        if DEBUG:
            dma("sync", "outS", dbg["dbg_rstd"], rstd_row[:, :],
                [("dve", rrv)])
        bb0 = bcast(rstd_row[:, 0:512], bc0, [("dve", rrv)])
        bb1 = bcast(rstd_row[:, 512:1024], bc1, [("dve", rrv)])
        for oc in range(DC):
            bi0, bi1 = (2 * oc) % 4, (2 * oc + 1) % 4
            pv = 0
            for hp in range(2):
                pw = load_panel(hp, out_p[2 * oc + hp], cast=True)
                for k in range(16):
                    hd = hp * 16 + k
                    for half, bi in ((0, bi0), (1, bi1)):
                        pv = mm(bi, pmain[bi][:, :],
                                wp[hp][:, k * 128:(k + 1) * 128],
                                attg_t(hd)[:, half * 512:(half + 1) * 512],
                                start=(hd == 0), stop=(hd == DC - 1),
                                waits=[pw, ("dve", attn_done)],
                                inc=(k == 15 and half == 1))
                st["wp_free"][hp] = [("pe", pv)]
            for half, bi, bcr, bv in ((0, bi0, bc0, bb0), (1, bi1, bc1, bb1)):
                ti = (2 * oc + half) % 2
                tv = E("vector",
                       lambda v, b=bi, t=ti, r=bcr: v.tensor_tensor(
                           t_ev[t][:, :], pmain[b][:, :], r[:, :], ALU.mult),
                       [("pe", pv), ("dve", bv)] + st["tev_free"][ti],
                       inc="dve")
                bank_done(bi, "dve", tv)
                st["acc_inst"][ti] += 1
                dma("gpsimd", f"accS{ti}",
                    x_dram.ap()[:, oc * T + half * 512:
                                oc * T + (half + 1) * 512],
                    t_ev[ti][:, :],
                    [("dve", tv), ("spillS", SPILL)],
                    accum_op=ALU.add)
                st["tev_free"][ti] = [(f"accS{ti}", 16 * st["acc_inst"][ti])]
        ACC0 = sc.tot("accS0")
        ACC1 = sc.tot("accS1")
        if DEBUG:
            dma("sync", "outS", dbg["dbg_x2"], x_dram.ap()[:, :],
                [("accS0", ACC0), ("accS1", ACC1)])
        if phases <= 4:
            _finish()
            return nc

        for half in range(2):
            for i in range(4):
                dma("sync", "xhS",
                    xh.ap().rearrange("p (c n) -> p c n", n=512)[
                        :, i * 8:(i + 1) * 8, :],
                    x_dram.ap().rearrange("p (c n) -> p c n", n=T)[
                        :, i * 8:(i + 1) * 8,
                        half * 512:(half + 1) * 512],
                    [("accS0", ACC0), ("accS1", ACC1),
                     ("pe", sc.tot("pe")), ("dve", sc.tot("dve"))])
            XH = sc.tot("xhS")

            def xh_t(dc):
                return xh[:, dc * 512:(dc + 1) * 512]

            def ffn2_out(oc, pv, bi, h=half):
                ti = oc % 2
                av = E("scalar",
                       lambda s, o=oc, b=bi, t=ti: s.activation(
                           t_ev[t][:, :], pmain[b][:, :], AF.Identity,
                           bias=b_half[:, o:o + 1], scale=0.5),
                       [("pe", pv), ("dve", c_bh)] + st["tev_free"][ti],
                       inc="act")
                bank_done(bi, "act", av)
                xv = E("vector",
                       lambda v, o=oc, t=ti: v.tensor_tensor(
                           xh_t(o), xh_t(o), t_ev[t][:, :], ALU.add),
                       [("act", av)], inc="dve")
                st["tev_free"][ti] = [("dve", xv)]
                dma("sync", "outS",
                    y_out[128 * oc:128 * (oc + 1), h * 512:(h + 1) * 512],
                    xh_t(oc), [("dve", xv)])

            ffn_half(xh_t, 0,
                     lambda dc: y3h[:, dc * 512:(dc + 1) * 512],
                     ffn2_out,
                     stats_extra=lambda dc, x_=XH: [("xhS", x_)])

        E("gpsimd", lambda g: None,
          [(nm, sc.tot(nm)) for nm in semnames if sc.tot(nm) > 0])

        block = e(nc.Block())
        for eng_name in ENGINES:
            ops = sc.ops[eng_name]
            if not ops:
                continue

            def body(engo, _ops=ops):
                for (ws, fn, inc) in _ops:
                    for (sname, v) in ws:
                        engo.wait_ge(sems[sname], v)
                    instr = fn(engo)
                    if inc is not None and instr is not None:
                        instr.then_inc(sems[inc[0]], inc[1])

            getattr(block, eng_name)(body)

    return nc


_CACHED = {}


def _host_inputs(inputs):
    qkv_w = np.asarray(inputs["L0_attn_qkv_w"], dtype=np.float32)
    out_w = np.asarray(inputs["L0_attn_out_w"], dtype=np.float32)
    gate_w = np.asarray(inputs["L0_attn_gate_w"], dtype=np.float32)
    norm_w = np.asarray(inputs["L0_attn_norm_w"], dtype=np.float32)
    w_base = np.asarray(inputs["L0_ffn_W_base"], dtype=np.float32)
    b_base = np.asarray(inputs["L0_ffn_b_base"], dtype=np.float32)
    enc0 = np.asarray(inputs["L0_ffn_enc0"], dtype=np.float32)
    enc1 = np.asarray(inputs["L0_ffn_enc1"], dtype=np.float32)
    coefs0 = np.asarray(inputs["L0_ffn_coefs0"], dtype=np.float32)
    coefs1 = np.asarray(inputs["L0_ffn_coefs1"], dtype=np.float32)
    mask0 = np.asarray(inputs["L0_ffn_mask0"])
    mask1 = np.asarray(inputs["L0_ffn_mask1"])

    c01 = np.concatenate([coefs0, coefs1], axis=2)
    m01 = np.concatenate([mask0, mask1], axis=2)
    shared = {
        "qkv_p": _panelize(qkv_w, 3 * HEADS * HD, D),
        "gate_p": _panelize(gate_w, D, D),
        "out_p": _panelize(out_w, D, D),
        "wb_p": _panelize(w_base, D, D),
        "mt_p": _panelize(_fourier_m().astype(np.float32), D, D).astype(
            ml_dtypes.bfloat16),
        "enc01t": np.ascontiguousarray(
            np.concatenate([enc0, enc1], axis=0).T),
        "wcoef": np.ascontiguousarray(
            c01.transpose(1, 0, 2).reshape(D, NE * RR)),
        "wmask": np.ascontiguousarray(
            m01.transpose(1, 0, 2).reshape(D, NE * RR).astype(np.uint8)),
        "bt_in": np.ascontiguousarray(b_base.reshape(DC, 128).T),
        "nw_in": np.ascontiguousarray(norm_w.reshape(DC, 128).T),
        "kdec_in": _kdecay_table(),
        "ident_in": np.eye(128, dtype=np.float32).astype(ml_dtypes.bfloat16),
        "ones_in": np.ones((128, 2), dtype=np.float32),
        "onesb_in": np.ones((128, 1), dtype=np.float32).astype(
            ml_dtypes.bfloat16),
        "onesrow_in": np.ones((1, 128), dtype=np.float32),
        "dker_in": _decay_tables(),
    }
    x = np.asarray(inputs["L0_attn_hidden_states"], dtype=np.float32)
    xr = x.reshape(B, CPS, S // CPS, D)
    in_maps = []
    for c in range(NCORE):
        m = dict(shared)
        m["xT"] = np.ascontiguousarray(xr[c // CPS, c % CPS].T)
        m["prefw_in"] = _prefix_weights(c)
        in_maps.append(m)
    return in_maps


def kernel(**inputs):
    in_maps = _host_inputs(inputs)
    if "nc" not in _CACHED:
        _CACHED["nc"] = build_program()
    res = run_bass_kernel_spmd(_CACHED["nc"], in_maps,
                               core_ids=list(range(NCORE)))
    out = np.empty((B, CPS, S // CPS, D), dtype=np.float32)
    for c in range(NCORE):
        out[c // CPS, c % CPS] = res.results[c]["y_out"].T
    return out.reshape(B, S, D)


if __name__ == "__main__":
    nc = build_program()
    print("build ok", sum(1 for _ in nc.all_instructions()))


# version 06
# speedup vs baseline: 1.0635x; 1.0635x over previous
import math
import numpy as np
import ml_dtypes
from contextlib import ExitStack

import concourse.bass as bass
import concourse.mybir as mybir
from concourse.bass_utils import run_bass_kernel_spmd

F32 = mybir.dt.float32
BF16 = mybir.dt.bfloat16
U8 = mybir.dt.uint8
AF = mybir.ActivationFunctionType
ALU = mybir.AluOpType

B, S, D = 2, 4096, 4096
HEADS, HD, BLK = 32, 128, 256
NCORE = 8
T = (B * S) // NCORE
DC = D // 128
NB = T // BLK
CPS = NCORE // B
EPS = 1e-5
RR = 40
NE = 3
DEBUG = False
ATTN_HEADS = HEADS
USE_CC = True

ENGINES = ("tensor", "vector", "scalar", "sync", "gpsimd")


def _slopes():
    base = 1.0 / 2.0 ** (8.0 / HEADS)
    factor = 1.0 - 1.0 / (HEADS - 1 + 1e-5) + 1e-5
    return np.array([base ** (h + 1.0) * factor for h in range(HEADS)],
                    dtype=np.float64)


SLOPES = _slopes()
BLOCK_DECAY = np.exp(-SLOPES * BLK)


def _panelize(w, o_dim, d_dim):
    oc = o_dim // 128
    t = w.reshape(oc, 128, d_dim // 128, 128).transpose(0, 2, 3, 1)
    t = t.reshape(oc, 2, 16, 128, 128).transpose(0, 1, 3, 2, 4)
    return np.ascontiguousarray(t.reshape(oc * 2, 128, 2048))


def _fourier_m():
    n = D
    t = np.arange(n, dtype=np.float64)
    M = np.empty((n, n), dtype=np.float64)
    M[:, 0] = 1.0 / math.sqrt(n)
    f = np.arange(1, n // 2, dtype=np.float64)
    ang = 2.0 * np.pi * np.outer(t, f) / n
    M[:, 1:n - 1:2] = math.sqrt(2.0 / n) * np.cos(ang)
    M[:, 2:n - 1:2] = math.sqrt(2.0 / n) * np.sin(ang)
    M[:, n - 1] = ((-1.0) ** t) / math.sqrt(n)
    return M


def _decay_tables():
    out = np.zeros((HEADS, 128, 768), dtype=np.float64)
    nidx = np.arange(BLK, dtype=np.float64)
    for h in range(HEADS):
        sl = SLOPES[h]
        for mc in range(2):
            m = (mc * 128 + np.arange(128, dtype=np.float64))[:, None]
            out[h, :, mc * 256:(mc + 1) * 256] = np.where(
                nidx[None, :] >= m, np.exp(-sl * (nidx[None, :] - m)), 0.0)
        out[h, :, 512:768] = np.exp(-sl * (nidx + 1.0))[None, :]
    return out.astype(np.float32)


def _kdecay_table():
    m = np.arange(BLK, dtype=np.float64)[:, None]
    return np.exp(-SLOPES[None, :] * (BLK - 1 - m)).astype(np.float32)


def _prefix_weights(core):
    p = core % CPS
    out = np.zeros((128, HEADS * CPS), dtype=np.float64)
    for h in range(HEADS):
        for r in range(CPS):
            if r < p:
                out[:, h * CPS + r] = BLOCK_DECAY[h] ** (NB * (p - 1 - r))
    return out.astype(np.float32)


class Sched:
    def __init__(self):
        self.ops = {e: [] for e in ENGINES}
        self.count = {}
        self.guar = {e: {} for e in ENGINES}

    def emit(self, eng, fn, waits=(), inc=None, inck=1):
        ws = []
        for (s, v) in waits:
            if v is None or v <= 0:
                continue
            have = self.count.get(s, 0)
            assert v <= have, f"[{eng}] wait {s}>={v} but only {have} emitted"
            if self.guar[eng].get(s, 0) >= v:
                continue
            self.guar[eng][s] = v
            ws.append((s, v))
        newv = None
        if inc is not None:
            self.count[inc] = self.count.get(inc, 0) + inck
            newv = self.count[inc]
            self.guar[eng][inc] = newv
        self.ops[eng].append((ws, fn, (inc, inck) if inc else None))
        return newv

    def tot(self, sem):
        return self.count.get(sem, 0)


class Bank:
    def __init__(self, tile):
        self.tile = tile
        self.deps = []


def build_program(phases=9):
    nc = bass.Bass("TRN2", target_bir_lowering=False)
    sc = Sched()

    def din(name, shape, dtype=F32):
        return nc.dram_tensor(name, list(shape), dtype,
                              kind="ExternalInput").ap()

    xT = din("xT", (D, T))
    qkv_p = din("qkv_p", (192, 128, 2048))
    gate_p = din("gate_p", (64, 128, 2048))
    out_p = din("out_p", (64, 128, 2048))
    wb_p = din("wb_p", (64, 128, 2048))
    mt_p = din("mt_p", (64, 128, 2048), BF16)
    enc01t_in = din("enc01t", (D, RR))
    wcoef = din("wcoef", (D, NE * RR))
    wmask = din("wmask", (D, NE * RR), U8)
    bt_in = din("bt_in", (128, DC))
    nw_in = din("nw_in", (128, DC))
    kdec_in = din("kdec_in", (BLK, HEADS))
    prefw_in = din("prefw_in", (128, 128))
    ident_in = din("ident_in", (128, 128), BF16)
    ones_in = din("ones_in", (128, 2))
    onesb_in = din("onesb_in", (128, 1), BF16)
    onesrow_in = din("onesrow_in", (1, 128))
    dker_in = din("dker_in", (HEADS, 128, 768))

    y_out = nc.dram_tensor("y_out", [D, T], F32, kind="ExternalOutput").ap()
    dbg = {}
    if DEBUG:
        dbg["dbg_x1"] = nc.dram_tensor("dbg_x1", [D, T], F32,
                                       kind="ExternalOutput").ap()
        dbg["dbg_y2"] = nc.dram_tensor("dbg_y2", [D, T], BF16,
                                       kind="ExternalOutput").ap()
        dbg["dbg_x2"] = nc.dram_tensor("dbg_x2", [128, DC * T], F32,
                                       kind="ExternalOutput").ap()
        dbg["dbg_ali"] = nc.dram_tensor("dbg_ali", [128, 65536], BF16,
                                        kind="ExternalOutput").ap()
        dbg["dbg_rstd"] = nc.dram_tensor("dbg_rstd", [1, T], F32,
                                         kind="ExternalOutput").ap()
        dbg["dbg_wef"] = nc.dram_tensor("dbg_wef", [RR, D], BF16,
                                        kind="ExternalOutput").ap()

    x_dram = nc.dram_tensor("x_dram", [128, DC * T], F32)
    x2_dram = nc.dram_tensor("x2_dram", [128, DC * T], F32)
    cc_in = [nc.dram_tensor(f"cc_in{h}", [128, 128], F32)
             for h in range(HEADS)]
    cc_out = [nc.dram_tensor(f"cc_out{h}", [CPS * 128, 128], F32)
              for h in range(HEADS)]

    with ExitStack() as es:
        e = es.enter_context

        cur = [(nc.sbuf_base + 31) // 32 * 32]

        def alloc(name, shape, dtype):
            esz = {F32: 4, BF16: 2, U8: 1}[dtype]
            off = (cur[0] + 31) // 32 * 32
            cur[0] = off + shape[1] * esz
            return nc.alloc_sbuf_tensor_at(name, list(shape), dtype,
                                           offset=off)

        ident = alloc("ident", (128, 128), BF16)
        onesb = alloc("onesb", (128, 1), BF16)
        ones32t = alloc("ones32", (128, 2), F32)
        onesrow = alloc("onesrow", (1, 128), F32)
        b_half = alloc("b_half", (128, DC), F32)
        nw = alloc("nw", (128, DC), F32)
        kdecst = alloc("kdecst", (128, 2 * HEADS), F32)
        prefw = alloc("prefw", (128, 128), F32)
        enc01T = alloc("enc01T", (128, DC * RR), BF16)
        m_sb = alloc("m_sb", (1, 512), F32)
        rstd_sb = alloc("rstd_sb", (1, 512), F32)
        sumsq_row = alloc("sumsq_row", (1, T), F32)
        rstd_row = alloc("rstd_row", (1, T), F32)
        bc0 = alloc("bc0", (128, 512), F32)
        bc1 = alloc("bc1", (128, 512), F32)
        e01_sb = alloc("e01_sb", (RR, 512), BF16)
        sqscr = alloc("sqscr", (128, 512), BF16)
        xbscr = alloc("xbscr", (128, 512), BF16)
        att2 = alloc("att2", (128, 256), BF16)
        wp = [alloc("wp0", (128, 2048), BF16), alloc("wp1", (128, 2048), BF16)]
        t_ev = [alloc("t_ev0", (128, 512), F32),
                alloc("t_ev1", (128, 512), F32)]
        weff01T = alloc("weff01T", (RR, D), BF16)
        yf32 = alloc("yf32", (128, 512), F32)

        offA = (cur[0] + 31) // 32 * 32
        SZA = 32768

        def alloc_at(name, shape, dtype, off):
            esz = {F32: 4, BF16: 2, U8: 1}[dtype]
            assert off % 32 == 0 and off + shape[1] * esz <= offA + SZA, name
            return nc.alloc_sbuf_tensor_at(name, list(shape), dtype,
                                           offset=off)

        y1h = alloc_at("y1h", (128, DC * 512), BF16, offA)
        w01sum = alloc_at("w01sum", (128, DC * RR), BF16, offA)
        coefscr = alloc_at("coefscr", (128, NE * RR), F32, offA + 2592)
        maskscr = alloc_at("maskscr", (128, NE * RR), U8, offA + 3104)
        wef_s = alloc_at("wef_s", (128, RR), BF16, offA + 3232)
        bcA2 = alloc_at("bcA2", (128, 512), F32, offA)
        bcB2 = alloc_at("bcB2", (128, 512), F32, offA + 2048)
        m2_sb = alloc_at("m2_sb", (1, 512), F32, offA + 4096)
        r2_sb = alloc_at("r2_sb", (1, 512), F32, offA + 6144)
        qkvA = alloc_at("qkvA", (128, 3 * T), BF16, offA)
        qkvB = alloc_at("qkvB", (128, 3 * T), BF16, offA + 6144)
        gatenw = alloc_at("gatenw", (128, T), F32, offA + 12288)
        vT = alloc_at("vT", (128, NB * 256), BF16, offA + 16384)
        ktT = alloc_at("ktT", (128, 256), BF16, offA + 18432)
        sdecT = alloc_at("sdecT", (128, 512), BF16, offA + 18944)
        qdecs = alloc_at("qdecs", (128, 256), BF16, offA + 19968)
        dker = [alloc_at("dker0", (128, 768), F32, offA + 20480),
                alloc_at("dker1", (128, 768), F32, offA + 23552)]
        kvf = alloc_at("kvf", (128, 128), F32, offA + 26624)
        kvb = alloc_at("kvb", (128, 128), BF16, offA + 27136)
        Ah = alloc_at("Ah", (128, 128), F32, offA + 27392)
        Agath = alloc_at("Agath", (128, CPS * 128), F32, offA + 27904)
        Cst = alloc_at("Cst", (128, NB * 128), F32, offA + 29952)
        kdecT = alloc_at("kdecT", (128, 256), BF16, offA + 32000)
        sc128 = alloc_at("sc128", (128, 128), F32, offA + 32000)

        offX = offA + SZA
        x_ar = nc.alloc_sbuf_tensor_at("x_ar", [128, DC * T], F32,
                                       offset=offX)
        xali = nc.alloc_sbuf_tensor_at("xali", [128, 65536], BF16,
                                       offset=offX)
        xh = nc.alloc_sbuf_tensor_at("xh", [128, DC * 512], F32, offset=offX)
        y3h = nc.alloc_sbuf_tensor_at("y3h", [128, DC * 512], BF16,
                                      offset=offX + 65536)
        assert offX + 131072 <= nc.sbuf_top, \
            f"SBUF overflow: end {offX + 131072} > top {nc.sbuf_top}"

        def y2_t(dc):
            return xali[:, dc * 2048: dc * 2048 + 1024]

        def attg_t(h):
            return xali[:, h * 2048 + 1024: (h + 1) * 2048]

        def x_t(dc):
            return x_ar[:, dc * T: (dc + 1) * T]

        pmain = [e(nc.psum_tensor(f"p{i}", [128, 512], F32))
                 for i in range(4)]
        pT = e(nc.psum_tensor("pT", [128, 1024], BF16))
        pE = e(nc.psum_tensor("pE", [128, 512], F32))
        pR1 = e(nc.psum_tensor("pR1", [1, 512], F32))
        pR2 = e(nc.psum_tensor("pR2", [1, 512], F32))
        banks = [Bank(t) for t in pmain] + [Bank(pT), Bank(pE), Bank(pR1),
                                            Bank(pR2)]
        BK_T, BK_E, BK_R1, BK_R2 = 4, 5, 6, 7

        sems = {}
        semnames = ["dma_k", "cofS", "xldS", "spillS", "wpS0", "wpS1", "dkS0",
                    "dkS1", "csS", "agS", "trS", "trC", "accS0", "accS1",
                    "xhS", "outS", "gkS", "accD", "pe", "dve", "act"]
        for nm in semnames:
            sems[nm] = e(nc.semaphore(nm))
        for h in range(HEADS):
            nm = f"cc{h}"
            sems[nm] = e(nc.semaphore(nm))
            semnames.append(nm)

        def E(eng, fn, waits=(), inc=None, inck=1):
            return sc.emit(eng, fn, waits, inc, inck)

        def dma(q, sem, out_ap, in_ap, waits=(), **kw):
            return E(q, lambda g: g.dma_start(out=out_ap, in_=in_ap, **kw),
                     waits, inc=sem, inck=16)

        def mm(bank_i, out_ap, lhsT, rhs, start, stop, waits=(), inc=False):
            b = banks[bank_i]
            w = list(waits) + (b.deps if start else [])
            if start:
                b.deps = []
            return E("tensor",
                     lambda t: t.matmul(out_ap, lhsT, rhs, start=start,
                                        stop=stop),
                     w, inc="pe" if (stop or inc) else None)

        def bank_done(bank_i, sem, val):
            banks[bank_i].deps = [(sem, val)]

        def bank_add(bank_i, sem, val):
            banks[bank_i].deps.append((sem, val))

        st = {"wp_free": [[], []], "wp_inst": [0, 0],
              "tev_free": [[], []], "acc_inst": [0, 0]}

        def load_panel(pi, src_ap, cast):
            st["wp_inst"][pi] += 1
            sem = f"wpS{pi}"
            dma("gpsimd" if cast else "sync", sem, wp[pi][:, :], src_ap,
                waits=st["wp_free"][pi])
            return (sem, 16 * st["wp_inst"][pi])

        dma("sync", "dma_k", ident[:, :], ident_in)
        dma("sync", "dma_k", b_half[:, :], bt_in)
        dma("sync", "dma_k", nw[:, :], nw_in)
        dma("sync", "dma_k", kdecst[:, 0:HEADS], kdec_in[0:128, :])
        dma("sync", "dma_k", kdecst[:, HEADS:2 * HEADS], kdec_in[128:256, :])
        dma("sync", "dma_k", prefw[:, :], prefw_in)
        dma("sync", "dma_k", ones32t[:, :], ones_in)
        dma("sync", "dma_k", onesb[:, :], onesb_in)
        dma("sync", "dma_k", onesrow[:, :], onesrow_in)
        K_CONST = sc.tot("dma_k")
        gk = dma("gpsimd", "gkS",
                 enc01T.ap().rearrange("p (c r) -> p c r", r=RR),
                 enc01t_in.rearrange("(c p) r -> p c r", p=128))
        c_bh = E("vector",
                 lambda v: v.tensor_scalar(b_half[:, :], b_half[:, :], 0.5,
                                           None, ALU.mult),
                 [("dma_k", K_CONST)], inc="dve")

        w01_done = 0
        for dc in range(DC):
            dma("sync", "cofS", coefscr[:, :],
                wcoef[dc * 128:(dc + 1) * 128, :], waits=[("dve", w01_done)])
            dma("sync", "cofS", maskscr[:, :],
                wmask[dc * 128:(dc + 1) * 128, :], waits=[("dve", w01_done)])
            kc = sc.tot("cofS")
            E("vector",
              lambda v: v.tensor_tensor(coefscr[:, :], coefscr[:, :],
                                        maskscr[:, :], ALU.mult),
              [("cofS", kc)], inc="dve")
            E("vector",
              lambda v: v.tensor_tensor(coefscr[:, 0:RR], coefscr[:, 0:RR],
                                        coefscr[:, RR:2 * RR], ALU.add),
              inc="dve")
            w01_done = E("vector",
                         lambda v, d=dc: v.tensor_tensor(
                             w01sum[:, d * RR:(d + 1) * RR],
                             coefscr[:, 0:RR], coefscr[:, 2 * RR:3 * RR],
                             ALU.add),
                         inc="dve")

        weff_done = 0
        for oc in range(DC):
            pv = 0
            for hp in range(2):
                pw = load_panel(hp, mt_p[2 * oc + hp], cast=False)
                for k in range(16):
                    dcc = hp * 16 + k
                    pv = mm(BK_E, pE[:, 0:RR],
                            wp[hp][:, k * 128:(k + 1) * 128],
                            w01sum[:, dcc * RR:(dcc + 1) * RR],
                            start=(dcc == 0), stop=(dcc == DC - 1),
                            waits=[pw, ("dve", w01_done)], inc=(k == 15))
                st["wp_free"][hp] = [("pe", pv)]
            ev = E("vector",
                   lambda v: v.tensor_scalar(wef_s[:, :], pE[:, 0:RR], 0.5,
                                             None, ALU.mult),
                   [("pe", pv)], inc="dve")
            bank_done(BK_E, "dve", ev)
            tv = E("tensor",
                   lambda t: t.transpose(pT[0:RR, 0:128], wef_s[:, :],
                                         ident[:, :]),
                   [("dve", ev), ("dma_k", K_CONST)] + banks[BK_T].deps,
                   inc="pe")
            banks[BK_T].deps = []
            weff_done = E("vector",
                          lambda v, o=oc: v.tensor_copy(
                              weff01T[:, o * 128:(o + 1) * 128],
                              pT[0:RR, 0:128]),
                          [("pe", tv)], inc="dve")
            bank_done(BK_T, "dve", weff_done)
        if DEBUG:
            dma("sync", "outS", dbg["dbg_wef"], weff01T[:, :],
                [("dve", weff_done)])

        for i in range(8):
            dma("sync", "xldS",
                x_ar.ap().rearrange("p (c n) -> p c n", n=T)[
                    :, i * 4:(i + 1) * 4, :],
                xT.rearrange("(c p) n -> p c n", p=128)[
                    :, i * 4:(i + 1) * 4, :])
        XLD = sc.tot("xldS")

        def _finish():
            for i in range(8):
                dma("sync", "outS",
                    y_out.rearrange("(c p) n -> p c n", p=128)[
                        :, i * 4:(i + 1) * 4, :],
                    x_ar.ap().rearrange("p (c n) -> p c n", n=T)[
                        :, i * 4:(i + 1) * 4, :],
                    [("dve", sc.tot("dve")), ("act", sc.tot("act")),
                     ("pe", sc.tot("pe")), ("xldS", XLD)])
            E("gpsimd", lambda g: None,
              [(nm, sc.tot(nm)) for nm in semnames if sc.tot(nm) > 0])
            block = e(nc.Block())
            for eng_name in ENGINES:
                ops = sc.ops[eng_name]
                if not ops:
                    continue

                def body(engo, _ops=ops):
                    for (ws, fn, inc) in _ops:
                        for (sname, v) in ws:
                            engo.wait_ge(sems[sname], v)
                        instr = fn(engo)
                        if inc is not None and instr is not None:
                            instr.then_inc(sems[inc[0]], inc[1])

                getattr(block, eng_name)(body)

        if phases <= 0:
            _finish()
            return nc

        def ln_stats(src, col0, m_dst, r_dst, extra_waits):
            r1v, r2v = 0, 0
            for dc in range(DC):
                w = list(extra_waits(dc))
                cv = E("vector",
                       lambda v, ap=src(dc): v.tensor_copy(
                           xbscr[:, :], ap[:, col0:col0 + 512]),
                       w + [("pe", r1v)], inc="dve")
                sv = E("scalar",
                       lambda s, ap=src(dc): s.activation(
                           sqscr[:, :], ap[:, col0:col0 + 512], AF.Square),
                       w + [("pe", r2v)], inc="act")
                r1v = mm(BK_R1, pR1[:, :], onesb[:, :], xbscr[:, :],
                         start=(dc == 0), stop=(dc == DC - 1),
                         waits=[("dve", cv), ("dma_k", K_CONST)], inc=True)
                r2v = mm(BK_R2, pR2[:, :], onesb[:, :], sqscr[:, :],
                         start=(dc == 0), stop=(dc == DC - 1),
                         waits=[("act", sv)], inc=True)
            mv = E("vector",
                   lambda v: v.tensor_scalar(m_dst[:, :], pR1[:, :],
                                             1.0 / D, None, ALU.mult),
                   [("pe", r2v)], inc="dve")
            E("vector",
              lambda v: v.tensor_scalar(r_dst[:, :], pR2[:, :], 1.0 / D,
                                        EPS, ALU.mult, ALU.add),
              inc="dve")
            E("vector",
              lambda v: v.tensor_tensor(pR1[0:1, :], m_dst[:, :],
                                        m_dst[:, :], ALU.mult),
              inc="dve")
            vv = E("vector",
                   lambda v: v.tensor_tensor(r_dst[:, :], r_dst[:, :],
                                             pR1[0:1, :], ALU.subtract),
                   inc="dve")
            bank_done(BK_R1, "dve", vv)
            sv = E("scalar",
                   lambda s: s.activation(r_dst[:, :], r_dst[:, :], AF.Sqrt),
                   [("dve", vv)], inc="act")
            rv = E("vector", lambda v: v.reciprocal(r_dst[:, :], r_dst[:, :]),
                   [("act", sv)], inc="dve")
            bank_done(BK_R2, "dve", rv)
            return rv

        def bcast(row_ap, dst, waits):
            w = list(waits) + banks[BK_E].deps + [("dma_k", K_CONST)]
            banks[BK_E].deps = []
            pv = E("tensor",
                   lambda t: t.matmul(pE[:, :], onesrow[:, :], row_ap,
                                      start=True, stop=True),
                   w, inc="pe")
            cv = E("vector", lambda v: v.tensor_copy(dst[:, :], pE[:, :]),
                   [("pe", pv)], inc="dve")
            bank_done(BK_E, "dve", cv)
            return cv

        def e01_calc(ysrc, ywait):
            pv = 0
            for dc in range(DC):
                pv = mm(BK_E, pE[0:RR, :],
                        enc01T[:, dc * RR:(dc + 1) * RR], ysrc(dc),
                        start=(dc == 0), stop=(dc == DC - 1),
                        waits=[("dve", ywait), ("gkS", gk)])
            ev = E("vector",
                   lambda v: v.tensor_copy(e01_sb[:, :], pE[0:RR, :]),
                   [("pe", pv)], inc="dve")
            bank_done(BK_E, "dve", ev)
            return ev

        def ffn_half(src, src_col0, ydst, out_cb, stats_extra):
            pe_entry = sc.tot("pe")
            rv = ln_stats(src, src_col0, m_sb, rstd_sb, stats_extra)
            b0 = bcast(rstd_sb[:, :], bc0, [("dve", rv)])
            b1 = bcast(m_sb[:, :], bc1, [("dve", rv)])
            ylast = 0
            for dc in range(DC):
                E("vector",
                  lambda v, sa=src(dc): v.tensor_tensor(
                      yf32[:, :], sa[:, src_col0:src_col0 + 512], bc1[:, :],
                      ALU.subtract),
                  [("dve", b1), ("pe", pe_entry)], inc="dve")
                ylast = E("vector",
                          lambda v, yd=ydst(dc): v.tensor_tensor(
                              yd, yf32[:, :], bc0[:, :], ALU.mult),
                          [("dve", b0)], inc="dve")
            ev01 = e01_calc(ydst, ylast)
            for oc in range(DC):
                bi = oc % 4
                for hp in range(2):
                    pw = load_panel(hp, wb_p[2 * oc + hp], cast=True)
                    pvp = 0
                    for k in range(16):
                        dcc = hp * 16 + k
                        pvp = mm(bi, pmain[bi][:, :],
                                 wp[hp][:, k * 128:(k + 1) * 128], ydst(dcc),
                                 start=(dcc == 0), stop=False,
                                 waits=[pw, ("dve", ylast)], inc=(k == 15))
                    st["wp_free"][hp] = [("pe", pvp)]
                pv = mm(bi, pmain[bi][:, :],
                        weff01T[:, oc * 128:(oc + 1) * 128], e01_sb[:, :],
                        start=False, stop=True,
                        waits=[("dve", ev01), ("dve", weff_done)])
                out_cb(oc, pv, bi)

        def make_ffn1_out(half):
            def out_cb(oc, pv, bi):
                ti = oc % 2
                av = E("scalar",
                       lambda s, o=oc, b=bi, t=ti: s.activation(
                           t_ev[t][:, :], pmain[b][:, :], AF.Identity,
                           bias=b_half[:, o:o + 1], scale=0.5),
                       [("pe", pv), ("dve", c_bh)] + st["tev_free"][ti],
                       inc="act")
                bank_done(bi, "act", av)
                xv = E("vector",
                       lambda v, o=oc, t=ti, h=half: v.tensor_tensor(
                           x_t(o)[:, h * 512:(h + 1) * 512],
                           x_t(o)[:, h * 512:(h + 1) * 512],
                           t_ev[t][:, :], ALU.add),
                       [("act", av)], inc="dve")
                st["tev_free"][ti] = [("dve", xv)]
            return out_cb

        for half in range(2):
            ffn_half(x_t, half * 512,
                     lambda dc: y1h[:, dc * 512:(dc + 1) * 512],
                     make_ffn1_out(half),
                     stats_extra=lambda dc: [("xldS", XLD),
                                             ("dve", weff_done)])
        ffn1_done = sc.tot("dve")
        if DEBUG:
            for i in range(8):
                dma("sync", "outS",
                    dbg["dbg_x1"].rearrange("(c p) n -> p c n", p=128)[
                        :, i * 4:(i + 1) * 4, :],
                    x_ar.ap().rearrange("p (c n) -> p c n", n=T)[
                        :, i * 4:(i + 1) * 4, :],
                    [("dve", ffn1_done)])
        if phases <= 1:
            _finish()
            return nc

        rv0 = ln_stats(x_t, 0, m_sb, rstd_sb, lambda dc: [("dve", ffn1_done)])
        rv1 = ln_stats(x_t, 512, m2_sb, r2_sb, lambda dc: [])
        for dc in range(DC):
            dma("sync", "spillS", x_dram.ap()[:, dc * T:(dc + 1) * T],
                x_t(dc), [("dve", ffn1_done)])
        SPILL = sc.tot("spillS")
        bB0 = bcast(m_sb[:, :], bc1, [("dve", rv0)])
        bA0 = bcast(rstd_sb[:, :], bc0, [("dve", rv0)])
        bA1 = bcast(r2_sb[:, :], bcA2, [("dve", rv1)])
        bB1 = bcast(m2_sb[:, :], bcB2, [("dve", rv1)])
        y2_done = 0
        for dc in range(DC):
            for half, bR, bM, wv in ((0, bc0, bc1, bA0), (1, bcA2, bcB2, bB1)):
                E("vector",
                  lambda v, d=dc, h=half, m=bM: v.tensor_tensor(
                      yf32[:, :],
                      x_t(d)[:, h * 512:(h + 1) * 512], m[:, :],
                      ALU.subtract),
                  [("dve", wv), ("dve", bB1)], inc="dve")
                y2_done = E("vector",
                            lambda v, d=dc, h=half, r=bR: v.tensor_tensor(
                                y2_t(d)[:, h * 512:(h + 1) * 512],
                                yf32[:, :], r[:, :], ALU.mult),
                            [("spillS", SPILL)], inc="dve")
        if DEBUG:
            for dc in range(DC):
                dma("sync", "outS", dbg["dbg_y2"][dc * 128:(dc + 1) * 128, :],
                    y2_t(dc), [("dve", y2_done)])
        if phases <= 2:
            _finish()
            return nc

        qkv_store = [qkvA, qkvB]
        qs_free = [[("dve", y2_done)], [("dve", y2_done)]]
        dk_inst = [0, 0]
        dk_free = [[("dve", y2_done)], [("dve", y2_done)]]
        ah_free = [("dve", y2_done)]
        gn_free = [("dve", y2_done)]
        ag_free = [("dve", y2_done)]
        tr_gate = [("dve", y2_done), ("spillS", SPILL)]
        sq_init = True

        for h in range(ATTN_HEADS):
            si = h % 2
            store = qkv_store[si]
            dki = h % 2
            bd = float(BLOCK_DECAY[h])
            dk_inst[dki] += 1
            dma("scalar", f"dkS{dki}", dker[dki][:, :], dker_in[h],
                waits=dk_free[dki])
            cdk_w = (f"dkS{dki}", 16 * dk_inst[dki])
            qkv_done = 0
            for ot3 in range(3):
                ptile = 3 * h + ot3
                bi0, bi1 = (2 * ot3) % 4, (2 * ot3 + 1) % 4
                pv = 0
                for hp in range(2):
                    pw = load_panel(hp, qkv_p[2 * ptile + hp], cast=True)
                    for k in range(16):
                        dcc = hp * 16 + k
                        for half, bi in ((0, bi0), (1, bi1)):
                            pv = mm(bi, pmain[bi][:, :],
                                    wp[hp][:, k * 128:(k + 1) * 128],
                                    y2_t(dcc)[:, half * 512:(half + 1) * 512],
                                    start=(dcc == 0), stop=(dcc == DC - 1),
                                    waits=[pw, ("dve", y2_done)] +
                                          (qs_free[si] if (dcc == 0 and
                                           ot3 == 0 and half == 0) else []),
                                    inc=(k == 15 and half == 1))
                    st["wp_free"][hp] = [("pe", pv)]
                for half, bi in ((0, bi0), (1, bi1)):
                    qkv_done = E(
                        "scalar",
                        lambda s, b=bi,
                        ap=store[:, ot3 * T + half * 512:
                                 ot3 * T + (half + 1) * 512]:
                        s.activation(ap, pmain[b][:, :], AF.Silu),
                        [("pe", pv)], inc="act")
                    bank_done(bi, "act", qkv_done)
            for b in range(NB):
                for mc in range(2):
                    dma("sync", "trS", vT[:, b * 256 + mc * 128:
                                          b * 256 + (mc + 1) * 128],
                        store[:, 2 * T + b * 256 + mc * 128:
                              2 * T + b * 256 + (mc + 1) * 128],
                        tr_gate + [("act", qkv_done)], transpose=True)
            gpv = 0
            for k in range(16):
                pw = (load_panel(0, gate_p[2 * h], cast=True)
                      if k == 0 else pw)
                for half, bi in ((0, 0), (1, 1)):
                    gpv = mm(bi, pmain[bi][:, :],
                             wp[0][:, k * 128:(k + 1) * 128],
                             y2_t(k)[:, half * 512:(half + 1) * 512],
                             start=(k == 0), stop=False,
                             waits=[pw], inc=(k == 15 and half == 1))
            st["wp_free"][0] = [("pe", gpv)]
            cmm_prev = 0
            avm = 0
            for b in range(NB):
                for mc in range(2):
                    dma("sync", "trS", ktT[:, mc * 128:(mc + 1) * 128],
                        store[:, T + b * 256 + mc * 128:
                              T + b * 256 + (mc + 1) * 128],
                        tr_gate + [("act", qkv_done), ("pe", cmm_prev)],
                        transpose=True)
                trk_s = trk_c = sc.tot("trS")
                kv2 = 0
                for mc in range(2):
                    kv2 = E("vector",
                            lambda v, m=mc, hh=h: v.tensor_scalar(
                                kdecT[:, m * 128:(m + 1) * 128],
                                ktT[:, m * 128:(m + 1) * 128],
                                kdecst[:, m * HEADS + hh:m * HEADS + hh + 1],
                                None, ALU.mult),
                            [("trS", trk_s), ("dma_k", K_CONST)], inc="dve")
                pv = 0
                for mc in range(2):
                    pv = mm(BK_E, pE[:, 0:128],
                            kdecT[:, mc * 128:(mc + 1) * 128],
                            vT[:, b * 256 + mc * 128:b * 256 + (mc + 1) * 128],
                            start=(mc == 0), stop=(mc == 1),
                            waits=[("dve", kv2), ("trS", trk_c)])
                cmm_prev = pv
                cvv = E("vector",
                        lambda v, b_=b: v.tensor_copy(
                            Cst[:, b_ * 128:(b_ + 1) * 128], pE[:, 0:128]),
                        [("pe", pv)], inc="dve")
                bank_done(BK_E, "dve", cvv)
                if b == 0:
                    avm = E("vector",
                            lambda v: v.tensor_copy(Ah[:, :], Cst[:, 0:128]),
                            ah_free, inc="dve")
                else:
                    E("vector",
                      lambda v, d=bd: v.tensor_scalar(Ah[:, :], Ah[:, :], d,
                                                      None, ALU.mult),
                      inc="dve")
                    avm = E("vector",
                            lambda v, b_=b: v.tensor_tensor(
                                Ah[:, :], Ah[:, :],
                                Cst[:, b_ * 128:(b_ + 1) * 128], ALU.add),
                            inc="dve")
            cs = dma("scalar", "csS", cc_in[h].ap(), Ah[:, :],
                     [("dve", avm)])
            ah_free = [("csS", cs)]
            if USE_CC:
                E("gpsimd",
                  lambda g, h_=h: g.collective_compute(
                      "AllGather", ALU.bypass,
                      replica_groups=[[0, 1, 2, 3], [4, 5, 6, 7]],
                      ins=[cc_in[h_].ap().opt()],
                      outs=[cc_out[h_].ap().opt()]),
                  [("csS", cs)], inc=f"cc{h}")
            else:
                for r in range(CPS):
                    E("gpsimd",
                      lambda g, h_=h, r_=r: g.dma_start(
                          out=cc_out[h_].ap()[r_ * 128:(r_ + 1) * 128, :],
                          in_=cc_in[h_].ap()),
                      [("csS", cs)], inc=f"cc{h}", inck=16)
            pv = 0
            pw = load_panel(1, gate_p[2 * h + 1], cast=True)
            for k in range(16):
                dcc = 16 + k
                for half, bi in ((0, 0), (1, 1)):
                    pv = mm(bi, pmain[bi][:, :],
                            wp[1][:, k * 128:(k + 1) * 128],
                            y2_t(dcc)[:, half * 512:(half + 1) * 512],
                            start=False, stop=(dcc == DC - 1),
                            waits=[pw], inc=(k == 15 and half == 1))
            st["wp_free"][1] = [("pe", pv)]
            gv = 0
            for half, bi in ((0, 0), (1, 1)):
                gv = E("scalar",
                       lambda s, b=bi, hf=half: s.activation(
                           gatenw[:, hf * 512:(hf + 1) * 512],
                           pmain[b][:, :], AF.Sigmoid),
                       [("pe", pv)] + gn_free, inc="act")
                bank_done(bi, "act", gv)
            gnv = E("vector",
                    lambda v, hh=h: v.tensor_scalar(
                        gatenw[:, :], gatenw[:, :], nw[:, hh:hh + 1],
                        None, ALU.mult),
                    [("act", gv), ("dma_k", K_CONST)], inc="dve")
            ag = dma("scalar", "agS", Agath[:, :],
                     cc_out[h].ap().rearrange("(r p) e -> p r e", p=128),
                     [(f"cc{h}", 1)] + ag_free)
            E("vector",
              lambda v, hh=h: v.tensor_scalar(
                  kvf[:, :], Agath[:, 0:128],
                  prefw[:, hh * CPS:hh * CPS + 1], None, ALU.mult),
              [("agS", ag), ("dma_k", K_CONST)], inc="dve")
            for r in range(1, CPS):
                E("vector",
                  lambda v, r_=r, hh=h: v.tensor_scalar(
                      sc128[:, :], Agath[:, r_ * 128:(r_ + 1) * 128],
                      prefw[:, hh * CPS + r_:hh * CPS + r_ + 1],
                      None, ALU.mult),
                  inc="dve")
                E("vector",
                  lambda v: v.tensor_tensor(kvf[:, :], kvf[:, :],
                                            sc128[:, :], ALU.add),
                  inc="dve")
            kbv = E("vector", lambda v: v.tensor_copy(kvb[:, :], kvf[:, :]),
                    inc="dve")
            last_attg = 0
            sq_free = []
            for b in range(NB):
                spv = []
                for mc in range(2):
                    spv.append(mm(
                        mc, pmain[mc][:, 0:256],
                        store[:, T + b * 256 + mc * 128:
                              T + b * 256 + (mc + 1) * 128],
                        store[:, b * 256:(b + 1) * 256],
                        start=True, stop=True, waits=[("act", qkv_done)]))
                sdv = []
                for mc in range(2):
                    dv = E("vector",
                           lambda v, m=mc, dk=dki: v.tensor_tensor(
                               sdecT[:, m * 256:(m + 1) * 256],
                               pmain[m][:, 0:256],
                               dker[dk][:, m * 256:(m + 1) * 256], ALU.mult),
                           [("pe", spv[mc]), cdk_w] + sq_free, inc="dve")
                    bank_done(mc, "dve", dv)
                    sdv.append(dv)
                qv = E("vector",
                       lambda v, ap=store[:, b * 256:(b + 1) * 256],
                       dk=dki: v.tensor_tensor(
                           qdecs[:, :], ap, dker[dk][:, 512:768], ALU.mult),
                       [("act", qkv_done), cdk_w] + sq_free, inc="dve")
                abi = 2 + (b % 2)
                for mc in range(2):
                    mm(abi, pmain[abi][:, 0:256],
                       vT[:, b * 256 + mc * 128:b * 256 + (mc + 1) * 128],
                       sdecT[:, mc * 256:(mc + 1) * 256],
                       start=(mc == 0), stop=False,
                       waits=[("dve", sdv[mc])])
                apv = mm(abi, pmain[abi][:, 0:256], kvb[:, :],
                         qdecs[:, :], start=False, stop=True,
                         waits=[("dve", qv), ("dve", kbv)])
                if b < NB - 1:
                    E("vector",
                      lambda v, d=bd: v.tensor_scalar(
                          kvf[:, :], kvf[:, :], d, None, ALU.mult),
                      [("pe", apv)], inc="dve")
                    E("vector",
                      lambda v, b_=b: v.tensor_tensor(
                          kvf[:, :], kvf[:, :],
                          Cst[:, b_ * 128:(b_ + 1) * 128], ALU.add),
                      inc="dve")
                    kbv = E("vector",
                            lambda v: v.tensor_copy(kvb[:, :], kvf[:, :]),
                            inc="dve")
                last_attg = E("vector",
                              lambda v, a0=pmain[abi][:, 0:256],
                              a1=gatenw[:, b * 256:(b + 1) * 256],
                              ao=attg_t(h)[:, b * 256:(b + 1) * 256]:
                              v.tensor_tensor(ao, a0, a1, ALU.mult),
                              [("dve", gnv), ("pe", apv)] +
                              ([("spillS", SPILL)] if (h == 0 and b == 0)
                               else []),
                              inc="dve")
                sqv = E("scalar",
                        lambda s, a=abi: s.activation(
                            att2[:, :], pmain[a][:, 0:256], AF.Square),
                        [("pe", apv), ("dve", last_attg)] + sq_free,
                        inc="act")
                bank_done(abi, "dve", last_attg)
                bank_add(abi, "act", sqv)
                w = banks[BK_R1].deps
                banks[BK_R1].deps = []
                spq = E("tensor",
                        lambda t: t.matmul(pR1[0:1, 0:256], onesb[:, :],
                                           att2[:, :], start=True,
                                           stop=True),
                        [("act", sqv)] + w, inc="pe")
                if sq_init:
                    sqa = E("vector",
                            lambda v, b_=b: v.tensor_copy(
                                sumsq_row[:, b_ * 256:(b_ + 1) * 256],
                                pR1[0:1, 0:256]),
                            [("pe", spq)], inc="dve")
                else:
                    sqa = E("vector",
                            lambda v, b_=b: v.tensor_tensor(
                                sumsq_row[:, b_ * 256:(b_ + 1) * 256],
                                sumsq_row[:, b_ * 256:(b_ + 1) * 256],
                                pR1[0:1, 0:256], ALU.add),
                            [("pe", spq)], inc="dve")
                bank_done(BK_R1, "dve", sqa)
                sq_free = [("dve", sqa)]
            sq_init = False
            qs_free[si] = [("dve", sc.tot("dve"))]
            dk_free[dki] = [("dve", sc.tot("dve"))]
            gn_free = [("dve", last_attg)]
            ag_free = [("dve", kbv)]
            tr_gate = [("pe", sc.tot("pe"))]
        attn_done = sc.tot("dve")
        if DEBUG:
            dma("sync", "outS", dbg["dbg_ali"], xali[:, :],
                [("dve", attn_done)])
        if phases <= 3:
            _finish()
            return nc

        tsv = E("vector",
                lambda v: v.tensor_scalar(rstd_row[:, :], sumsq_row[:, :],
                                          1.0 / D, EPS, ALU.mult, ALU.add),
                [("dve", attn_done)], inc="dve")
        ssv = E("scalar",
                lambda s: s.activation(rstd_row[:, :], rstd_row[:, :],
                                       AF.Sqrt),
                [("dve", tsv)], inc="act")
        rrv = E("vector",
                lambda v: v.reciprocal(rstd_row[:, :], rstd_row[:, :]),
                [("act", ssv)], inc="dve")
        if DEBUG:
            dma("sync", "outS", dbg["dbg_rstd"], rstd_row[:, :],
                [("dve", rrv)])
        bb0 = bcast(rstd_row[:, 0:512], bc0, [("dve", rrv)])
        bb1 = bcast(rstd_row[:, 512:1024], bc1, [("dve", rrv)])
        for oc in range(DC):
            bi0, bi1 = (2 * oc) % 4, (2 * oc + 1) % 4
            pv = 0
            for hp in range(2):
                pw = load_panel(hp, out_p[2 * oc + hp], cast=True)
                for k in range(16):
                    hd = hp * 16 + k
                    for half, bi in ((0, bi0), (1, bi1)):
                        pv = mm(bi, pmain[bi][:, :],
                                wp[hp][:, k * 128:(k + 1) * 128],
                                attg_t(hd)[:, half * 512:(half + 1) * 512],
                                start=(hd == 0), stop=(hd == DC - 1),
                                waits=[pw, ("dve", attn_done)],
                                inc=(k == 15 and half == 1))
                st["wp_free"][hp] = [("pe", pv)]
            for half, bi, bcr, bv in ((0, bi0, bc0, bb0), (1, bi1, bc1, bb1)):
                ti = (2 * oc + half) % 2
                tv = E("vector",
                       lambda v, b=bi, t=ti, r=bcr: v.tensor_tensor(
                           t_ev[t][:, :], pmain[b][:, :], r[:, :], ALU.mult),
                       [("pe", pv), ("dve", bv)] + st["tev_free"][ti],
                       inc="dve")
                bank_done(bi, "dve", tv)
                st["acc_inst"][ti] += 1
                dma("sync", f"accS{ti}",
                    x2_dram.ap()[:, oc * T + half * 512:
                                 oc * T + (half + 1) * 512],
                    t_ev[ti][:, :], [("dve", tv)])
                st["tev_free"][ti] = [(f"accS{ti}", 16 * st["acc_inst"][ti])]
        ACC0 = sc.tot("accS0")
        ACC1 = sc.tot("accS1")
        if phases <= 4:
            _finish()
            return nc

        for half in range(2):
            for i in range(4):
                dma("sync", "xhS",
                    xh.ap().rearrange("p (c n) -> p c n", n=512)[
                        :, i * 8:(i + 1) * 8, :],
                    x_dram.ap().rearrange("p (c n) -> p c n", n=T)[
                        :, i * 8:(i + 1) * 8,
                        half * 512:(half + 1) * 512],
                    [("accS0", ACC0), ("accS1", ACC1), ("spillS", SPILL),
                     ("pe", sc.tot("pe")), ("dve", sc.tot("dve"))])
            XH = sc.tot("xhS")
            for i in range(4):
                dma("gpsimd", "accD",
                    xh.ap().rearrange("p (c n) -> p c n", n=512)[
                        :, i * 8:(i + 1) * 8, :],
                    x2_dram.ap().rearrange("p (c n) -> p c n", n=T)[
                        :, i * 8:(i + 1) * 8,
                        half * 512:(half + 1) * 512],
                    [("xhS", XH), ("accS0", ACC0), ("accS1", ACC1)],
                    accum_op=ALU.add)
            XHA = sc.tot("accD")

            def xh_t(dc):
                return xh[:, dc * 512:(dc + 1) * 512]

            def ffn2_out(oc, pv, bi, h=half):
                ti = oc % 2
                av = E("scalar",
                       lambda s, o=oc, b=bi, t=ti: s.activation(
                           t_ev[t][:, :], pmain[b][:, :], AF.Identity,
                           bias=b_half[:, o:o + 1], scale=0.5),
                       [("pe", pv), ("dve", c_bh)] + st["tev_free"][ti],
                       inc="act")
                bank_done(bi, "act", av)
                xv = E("vector",
                       lambda v, o=oc, t=ti: v.tensor_tensor(
                           xh_t(o), xh_t(o), t_ev[t][:, :], ALU.add),
                       [("act", av)], inc="dve")
                st["tev_free"][ti] = [("dve", xv)]
                dma("sync", "outS",
                    y_out[128 * oc:128 * (oc + 1), h * 512:(h + 1) * 512],
                    xh_t(oc), [("dve", xv)])

            ffn_half(xh_t, 0,
                     lambda dc: y3h[:, dc * 512:(dc + 1) * 512],
                     ffn2_out,
                     stats_extra=lambda dc, x_=XHA: [("accD", x_)])

        E("gpsimd", lambda g: None,
          [(nm, sc.tot(nm)) for nm in semnames if sc.tot(nm) > 0])

        block = e(nc.Block())
        for eng_name in ENGINES:
            ops = sc.ops[eng_name]
            if not ops:
                continue

            def body(engo, _ops=ops):
                for (ws, fn, inc) in _ops:
                    for (sname, v) in ws:
                        engo.wait_ge(sems[sname], v)
                    instr = fn(engo)
                    if inc is not None and instr is not None:
                        instr.then_inc(sems[inc[0]], inc[1])

            getattr(block, eng_name)(body)

    return nc


_CACHED = {}


def _host_inputs(inputs):
    qkv_w = np.asarray(inputs["L0_attn_qkv_w"], dtype=np.float32)
    out_w = np.asarray(inputs["L0_attn_out_w"], dtype=np.float32)
    gate_w = np.asarray(inputs["L0_attn_gate_w"], dtype=np.float32)
    norm_w = np.asarray(inputs["L0_attn_norm_w"], dtype=np.float32)
    w_base = np.asarray(inputs["L0_ffn_W_base"], dtype=np.float32)
    b_base = np.asarray(inputs["L0_ffn_b_base"], dtype=np.float32)
    enc0 = np.asarray(inputs["L0_ffn_enc0"], dtype=np.float32)
    enc1 = np.asarray(inputs["L0_ffn_enc1"], dtype=np.float32)
    coefs0 = np.asarray(inputs["L0_ffn_coefs0"], dtype=np.float32)
    coefs1 = np.asarray(inputs["L0_ffn_coefs1"], dtype=np.float32)
    mask0 = np.asarray(inputs["L0_ffn_mask0"])
    mask1 = np.asarray(inputs["L0_ffn_mask1"])

    c01 = np.concatenate([coefs0, coefs1], axis=2)
    m01 = np.concatenate([mask0, mask1], axis=2)
    shared = {
        "qkv_p": _panelize(qkv_w, 3 * HEADS * HD, D),
        "gate_p": _panelize(gate_w, D, D),
        "out_p": _panelize(out_w, D, D),
        "wb_p": _panelize(w_base, D, D),
        "mt_p": _panelize(_fourier_m().astype(np.float32), D, D).astype(
            ml_dtypes.bfloat16),
        "enc01t": np.ascontiguousarray(
            np.concatenate([enc0, enc1], axis=0).T),
        "wcoef": np.ascontiguousarray(
            c01.transpose(1, 0, 2).reshape(D, NE * RR)),
        "wmask": np.ascontiguousarray(
            m01.transpose(1, 0, 2).reshape(D, NE * RR).astype(np.uint8)),
        "bt_in": np.ascontiguousarray(b_base.reshape(DC, 128).T),
        "nw_in": np.ascontiguousarray(norm_w.reshape(DC, 128).T),
        "kdec_in": _kdecay_table(),
        "ident_in": np.eye(128, dtype=np.float32).astype(ml_dtypes.bfloat16),
        "ones_in": np.ones((128, 2), dtype=np.float32),
        "onesb_in": np.ones((128, 1), dtype=np.float32).astype(
            ml_dtypes.bfloat16),
        "onesrow_in": np.ones((1, 128), dtype=np.float32),
        "dker_in": _decay_tables(),
    }
    x = np.asarray(inputs["L0_attn_hidden_states"], dtype=np.float32)
    xr = x.reshape(B, CPS, S // CPS, D)
    in_maps = []
    for c in range(NCORE):
        m = dict(shared)
        m["xT"] = np.ascontiguousarray(xr[c // CPS, c % CPS].T)
        m["prefw_in"] = _prefix_weights(c)
        in_maps.append(m)
    return in_maps


def kernel(**inputs):
    in_maps = _host_inputs(inputs)
    if "nc" not in _CACHED:
        _CACHED["nc"] = build_program()
    res = run_bass_kernel_spmd(_CACHED["nc"], in_maps,
                               core_ids=list(range(NCORE)))
    out = np.empty((B, CPS, S // CPS, D), dtype=np.float32)
    for c in range(NCORE):
        out[c // CPS, c % CPS] = res.results[c]["y_out"].T
    return out.reshape(B, S, D)


if __name__ == "__main__":
    nc = build_program()
    print("build ok", sum(1 for _ in nc.all_instructions()))


# version 07 (final)
# speedup vs baseline: 1.1071x; 1.0410x over previous
import math
import numpy as np
import ml_dtypes
from contextlib import ExitStack

import concourse.bass as bass
import concourse.mybir as mybir
from concourse.bass_utils import run_bass_kernel_spmd

F32 = mybir.dt.float32
BF16 = mybir.dt.bfloat16
U8 = mybir.dt.uint8
AF = mybir.ActivationFunctionType
ALU = mybir.AluOpType

B, S, D = 2, 4096, 4096
HEADS, HD, BLK = 32, 128, 256
NCORE = 8
T = (B * S) // NCORE
DC = D // 128
NB = T // BLK
CPS = NCORE // B
EPS = 1e-5
RR = 40
NE = 3
DEBUG = False
ATTN_HEADS = HEADS
USE_CC = True

ENGINES = ("tensor", "vector", "scalar", "sync", "gpsimd")


def _slopes():
    base = 1.0 / 2.0 ** (8.0 / HEADS)
    factor = 1.0 - 1.0 / (HEADS - 1 + 1e-5) + 1e-5
    return np.array([base ** (h + 1.0) * factor for h in range(HEADS)],
                    dtype=np.float64)


SLOPES = _slopes()
BLOCK_DECAY = np.exp(-SLOPES * BLK)


def _panelize(w, o_dim, d_dim):
    oc = o_dim // 128
    t = w.reshape(oc, 128, d_dim // 128, 128).transpose(0, 2, 3, 1)
    t = t.reshape(oc, 2, 16, 128, 128).transpose(0, 1, 3, 2, 4)
    return np.ascontiguousarray(t.reshape(oc * 2, 128, 2048))


def _fourier_m():
    n = D
    t = np.arange(n, dtype=np.float64)
    M = np.empty((n, n), dtype=np.float64)
    M[:, 0] = 1.0 / math.sqrt(n)
    f = np.arange(1, n // 2, dtype=np.float64)
    ang = 2.0 * np.pi * np.outer(t, f) / n
    M[:, 1:n - 1:2] = math.sqrt(2.0 / n) * np.cos(ang)
    M[:, 2:n - 1:2] = math.sqrt(2.0 / n) * np.sin(ang)
    M[:, n - 1] = ((-1.0) ** t) / math.sqrt(n)
    return M


def _decay_tables():
    out = np.zeros((HEADS, 128, 768), dtype=np.float64)
    nidx = np.arange(BLK, dtype=np.float64)
    for h in range(HEADS):
        sl = SLOPES[h]
        for mc in range(2):
            m = (mc * 128 + np.arange(128, dtype=np.float64))[:, None]
            out[h, :, mc * 256:(mc + 1) * 256] = np.where(
                nidx[None, :] >= m, np.exp(-sl * (nidx[None, :] - m)), 0.0)
        out[h, :, 512:768] = np.exp(-sl * (nidx + 1.0))[None, :]
    return out.astype(np.float32)


def _kdecay_table():
    m = np.arange(BLK, dtype=np.float64)[:, None]
    return np.exp(-SLOPES[None, :] * (BLK - 1 - m)).astype(np.float32)


def _prefix_weights(core):
    p = core % CPS
    out = np.zeros((128, HEADS * CPS), dtype=np.float64)
    for h in range(HEADS):
        for r in range(CPS):
            if r < p:
                out[:, h * CPS + r] = BLOCK_DECAY[h] ** (NB * (p - 1 - r))
    return out.astype(np.float32)


class Sched:
    def __init__(self):
        self.ops = {e: [] for e in ENGINES}
        self.count = {}
        self.guar = {e: {} for e in ENGINES}

    def emit(self, eng, fn, waits=(), inc=None, inck=1):
        ws = []
        for (s, v) in waits:
            if v is None or v <= 0:
                continue
            have = self.count.get(s, 0)
            assert v <= have, f"[{eng}] wait {s}>={v} but only {have} emitted"
            if self.guar[eng].get(s, 0) >= v:
                continue
            self.guar[eng][s] = v
            ws.append((s, v))
        newv = None
        if inc is not None:
            self.count[inc] = self.count.get(inc, 0) + inck
            newv = self.count[inc]
            self.guar[eng][inc] = newv
        self.ops[eng].append((ws, fn, (inc, inck) if inc else None))
        return newv

    def tot(self, sem):
        return self.count.get(sem, 0)


class Bank:
    def __init__(self, tile):
        self.tile = tile
        self.deps = []


def build_program(phases=9):
    nc = bass.Bass("TRN2", target_bir_lowering=False)
    sc = Sched()

    def din(name, shape, dtype=F32):
        return nc.dram_tensor(name, list(shape), dtype,
                              kind="ExternalInput").ap()

    xT = din("xT", (D, T))
    qkv_p = din("qkv_p", (192, 128, 2048))
    gate_p = din("gate_p", (64, 128, 2048))
    out_p = din("out_p", (64, 128, 2048))
    wb_p = din("wb_p", (64, 128, 2048))
    mt_p = din("mt_p", (64, 128, 2048), BF16)
    enc01t_in = din("enc01t", (D, RR))
    wcoef = din("wcoef", (D, NE * RR))
    wmask = din("wmask", (D, NE * RR), U8)
    bt_in = din("bt_in", (128, DC))
    nw_in = din("nw_in", (128, DC))
    kdec_in = din("kdec_in", (BLK, HEADS))
    prefw_in = din("prefw_in", (128, 128))
    ident_in = din("ident_in", (128, 128), BF16)
    ones_in = din("ones_in", (128, 2))
    onesb_in = din("onesb_in", (128, 1), BF16)
    onesrow_in = din("onesrow_in", (1, 128))
    dker_in = din("dker_in", (HEADS, 128, 768))

    y_out = nc.dram_tensor("y_out", [D, T], F32, kind="ExternalOutput").ap()
    dbg = {}
    if DEBUG:
        dbg["dbg_x1"] = nc.dram_tensor("dbg_x1", [D, T], F32,
                                       kind="ExternalOutput").ap()
        dbg["dbg_y2"] = nc.dram_tensor("dbg_y2", [D, T], BF16,
                                       kind="ExternalOutput").ap()
        dbg["dbg_x2"] = nc.dram_tensor("dbg_x2", [128, DC * T], F32,
                                       kind="ExternalOutput").ap()
        dbg["dbg_ali"] = nc.dram_tensor("dbg_ali", [128, 65536], BF16,
                                        kind="ExternalOutput").ap()
        dbg["dbg_rstd"] = nc.dram_tensor("dbg_rstd", [1, T], F32,
                                         kind="ExternalOutput").ap()
        dbg["dbg_wef"] = nc.dram_tensor("dbg_wef", [RR, D], BF16,
                                        kind="ExternalOutput").ap()

    x_dram = nc.dram_tensor("x_dram", [128, DC * T], F32)
    x2_dram = nc.dram_tensor("x2_dram", [128, DC * T], F32)
    cc_in = [nc.dram_tensor(f"cc_in{h}", [128, 128], F32)
             for h in range(HEADS)]
    cc_out = [nc.dram_tensor(f"cc_out{h}", [CPS * 128, 128], F32)
              for h in range(HEADS)]

    with ExitStack() as es:
        e = es.enter_context

        cur = [(nc.sbuf_base + 31) // 32 * 32]

        def alloc(name, shape, dtype):
            esz = {F32: 4, BF16: 2, U8: 1}[dtype]
            off = (cur[0] + 31) // 32 * 32
            cur[0] = off + shape[1] * esz
            return nc.alloc_sbuf_tensor_at(name, list(shape), dtype,
                                           offset=off)

        ident = alloc("ident", (128, 128), BF16)
        onesb = alloc("onesb", (128, 1), BF16)
        ones32t = alloc("ones32", (128, 2), F32)
        onesrow = alloc("onesrow", (1, 128), F32)
        b_half = alloc("b_half", (128, DC), F32)
        nw = alloc("nw", (128, DC), F32)
        kdecst = alloc("kdecst", (128, 2 * HEADS), F32)
        prefw = alloc("prefw", (128, 128), F32)
        enc01T = alloc("enc01T", (128, DC * RR), BF16)
        m_sb = alloc("m_sb", (1, 512), F32)
        rstd_sb = alloc("rstd_sb", (1, 512), F32)
        sumsq_row = alloc("sumsq_row", (1, T), F32)
        rstd_row = alloc("rstd_row", (1, T), F32)
        bc0 = alloc("bc0", (128, 512), F32)
        bc1 = alloc("bc1", (128, 512), F32)
        e01_sb = alloc("e01_sb", (RR, 512), BF16)
        sqscr = alloc("sqscr", (128, 512), BF16)
        xbscr = alloc("xbscr", (128, 512), BF16)
        att2 = alloc("att2", (128, 256), BF16)
        wp = [alloc("wp0", (128, 2048), BF16), alloc("wp1", (128, 2048), BF16)]
        t_ev = [alloc("t_ev0", (128, 512), F32),
                alloc("t_ev1", (128, 512), F32)]
        weff01T = alloc("weff01T", (RR, D), BF16)
        yf32 = alloc("yf32", (128, 512), F32)

        offA = (cur[0] + 31) // 32 * 32
        SZA = 32768

        def alloc_at(name, shape, dtype, off):
            esz = {F32: 4, BF16: 2, U8: 1}[dtype]
            assert off % 32 == 0 and off + shape[1] * esz <= offA + SZA, name
            return nc.alloc_sbuf_tensor_at(name, list(shape), dtype,
                                           offset=off)

        y1h = alloc_at("y1h", (128, DC * 512), BF16, offA)
        w01sum = alloc_at("w01sum", (128, DC * RR), BF16, offA)
        coefscr = alloc_at("coefscr", (128, NE * RR), F32, offA + 2592)
        maskscr = alloc_at("maskscr", (128, NE * RR), U8, offA + 3104)
        wef_s = alloc_at("wef_s", (128, RR), BF16, offA + 3232)
        bcA2 = alloc_at("bcA2", (128, 512), F32, offA)
        bcB2 = alloc_at("bcB2", (128, 512), F32, offA + 2048)
        m2_sb = alloc_at("m2_sb", (1, 512), F32, offA + 4096)
        r2_sb = alloc_at("r2_sb", (1, 512), F32, offA + 6144)
        qkvA = alloc_at("qkvA", (128, 3 * T), BF16, offA)
        qkvB = alloc_at("qkvB", (128, 3 * T), BF16, offA + 6144)
        gatenw = alloc_at("gatenw", (128, T), F32, offA + 12288)
        vT = alloc_at("vT", (128, NB * 256), BF16, offA + 16384)
        kdecT2 = alloc_at("kdecT2", (128, 256), BF16, offA + 18432)
        sdecT = alloc_at("sdecT", (128, 512), BF16, offA + 18944)
        qdecs = alloc_at("qdecs", (128, 256), BF16, offA + 19968)
        dker = [alloc_at("dker0", (128, 768), F32, offA + 20480),
                alloc_at("dker1", (128, 768), F32, offA + 23552)]
        kvf = alloc_at("kvf", (128, 128), F32, offA + 26624)
        kvb = alloc_at("kvb", (128, 128), BF16, offA + 27136)
        Ah = alloc_at("Ah", (128, 128), F32, offA + 27392)
        Agath = alloc_at("Agath", (128, CPS * 128), F32, offA + 27904)
        Cst = alloc_at("Cst", (128, NB * 128), F32, offA + 29952)
        kdecT = alloc_at("kdecT", (128, 256), BF16, offA + 32000)
        sc128 = alloc_at("sc128", (128, 128), F32, offA + 32000)

        offX = offA + SZA
        x_ar = nc.alloc_sbuf_tensor_at("x_ar", [128, DC * T], F32,
                                       offset=offX)
        xali = nc.alloc_sbuf_tensor_at("xali", [128, 65536], BF16,
                                       offset=offX)
        xh = nc.alloc_sbuf_tensor_at("xh", [128, DC * 512], F32, offset=offX)
        y3h = nc.alloc_sbuf_tensor_at("y3h", [128, DC * 512], BF16,
                                      offset=offX + 65536)
        assert offX + 131072 <= nc.sbuf_top, \
            f"SBUF overflow: end {offX + 131072} > top {nc.sbuf_top}"

        def y2_t(dc):
            return xali[:, dc * 2048: dc * 2048 + 1024]

        def attg_t(h):
            return xali[:, h * 2048 + 1024: (h + 1) * 2048]

        def x_t(dc):
            return x_ar[:, dc * T: (dc + 1) * T]

        pmain = [e(nc.psum_tensor(f"p{i}", [128, 512], F32))
                 for i in range(4)]
        pT = e(nc.psum_tensor("pT", [128, 1024], BF16))
        pE = e(nc.psum_tensor("pE", [128, 512], F32))
        pR1 = e(nc.psum_tensor("pR1", [1, 512], F32))
        pR2 = e(nc.psum_tensor("pR2", [1, 512], F32))
        banks = [Bank(t) for t in pmain] + [Bank(pT), Bank(pE), Bank(pR1),
                                            Bank(pR2)]
        BK_T, BK_E, BK_R1, BK_R2 = 4, 5, 6, 7

        sems = {}
        semnames = ["dma_k", "cofS", "xldS", "spillS", "wpS0", "wpS1", "dkS0",
                    "dkS1", "csS", "agS", "trS", "trC", "accS0", "accS1",
                    "xhS", "outS", "gkS", "accD", "pe", "dve", "act"]
        for nm in semnames:
            sems[nm] = e(nc.semaphore(nm))
        for h in range(HEADS):
            nm = f"cc{h}"
            sems[nm] = e(nc.semaphore(nm))
            semnames.append(nm)

        def E(eng, fn, waits=(), inc=None, inck=1):
            return sc.emit(eng, fn, waits, inc, inck)

        def dma(q, sem, out_ap, in_ap, waits=(), **kw):
            return E(q, lambda g: g.dma_start(out=out_ap, in_=in_ap, **kw),
                     waits, inc=sem, inck=16)

        def mm(bank_i, out_ap, lhsT, rhs, start, stop, waits=(), inc=False):
            b = banks[bank_i]
            w = list(waits) + (b.deps if start else [])
            if start:
                b.deps = []
            return E("tensor",
                     lambda t: t.matmul(out_ap, lhsT, rhs, start=start,
                                        stop=stop),
                     w, inc="pe" if (stop or inc) else None)

        def bank_done(bank_i, sem, val):
            banks[bank_i].deps = [(sem, val)]

        def bank_add(bank_i, sem, val):
            banks[bank_i].deps.append((sem, val))

        st = {"wp_free": [[], []], "wp_inst": [0, 0],
              "tev_free": [[], []], "acc_inst": [0, 0]}

        def load_panel(pi, src_ap, cast):
            st["wp_inst"][pi] += 1
            sem = f"wpS{pi}"
            dma("gpsimd" if cast else "sync", sem, wp[pi][:, :], src_ap,
                waits=st["wp_free"][pi])
            return (sem, 16 * st["wp_inst"][pi])

        dma("sync", "dma_k", ident[:, :], ident_in)
        dma("sync", "dma_k", b_half[:, :], bt_in)
        dma("sync", "dma_k", nw[:, :], nw_in)
        dma("sync", "dma_k", kdecst[:, 0:HEADS], kdec_in[0:128, :])
        dma("sync", "dma_k", kdecst[:, HEADS:2 * HEADS], kdec_in[128:256, :])
        dma("sync", "dma_k", prefw[:, :], prefw_in)
        dma("sync", "dma_k", ones32t[:, :], ones_in)
        dma("sync", "dma_k", onesb[:, :], onesb_in)
        dma("sync", "dma_k", onesrow[:, :], onesrow_in)
        K_CONST = sc.tot("dma_k")
        gk = dma("gpsimd", "gkS",
                 enc01T.ap().rearrange("p (c r) -> p c r", r=RR),
                 enc01t_in.rearrange("(c p) r -> p c r", p=128))
        c_bh = E("vector",
                 lambda v: v.tensor_scalar(b_half[:, :], b_half[:, :], 0.5,
                                           None, ALU.mult),
                 [("dma_k", K_CONST)], inc="dve")

        w01_done = 0
        for dc in range(DC):
            dma("sync", "cofS", coefscr[:, :],
                wcoef[dc * 128:(dc + 1) * 128, :], waits=[("dve", w01_done)])
            dma("sync", "cofS", maskscr[:, :],
                wmask[dc * 128:(dc + 1) * 128, :], waits=[("dve", w01_done)])
            kc = sc.tot("cofS")
            E("vector",
              lambda v: v.tensor_tensor(coefscr[:, :], coefscr[:, :],
                                        maskscr[:, :], ALU.mult),
              [("cofS", kc)], inc="dve")
            E("vector",
              lambda v: v.tensor_tensor(coefscr[:, 0:RR], coefscr[:, 0:RR],
                                        coefscr[:, RR:2 * RR], ALU.add),
              inc="dve")
            w01_done = E("vector",
                         lambda v, d=dc: v.tensor_tensor(
                             w01sum[:, d * RR:(d + 1) * RR],
                             coefscr[:, 0:RR], coefscr[:, 2 * RR:3 * RR],
                             ALU.add),
                         inc="dve")

        weff_done = 0
        for oc in range(DC):
            pv = 0
            for hp in range(2):
                pw = load_panel(hp, mt_p[2 * oc + hp], cast=False)
                for k in range(16):
                    dcc = hp * 16 + k
                    pv = mm(BK_E, pE[:, 0:RR],
                            wp[hp][:, k * 128:(k + 1) * 128],
                            w01sum[:, dcc * RR:(dcc + 1) * RR],
                            start=(dcc == 0), stop=(dcc == DC - 1),
                            waits=[pw, ("dve", w01_done)], inc=(k == 15))
                st["wp_free"][hp] = [("pe", pv)]
            ev = E("vector",
                   lambda v: v.tensor_scalar(wef_s[:, :], pE[:, 0:RR], 0.5,
                                             None, ALU.mult),
                   [("pe", pv)], inc="dve")
            bank_done(BK_E, "dve", ev)
            tv = E("tensor",
                   lambda t: t.transpose(pT[0:RR, 0:128], wef_s[:, :],
                                         ident[:, :]),
                   [("dve", ev), ("dma_k", K_CONST)] + banks[BK_T].deps,
                   inc="pe")
            banks[BK_T].deps = []
            weff_done = E("vector",
                          lambda v, o=oc: v.tensor_copy(
                              weff01T[:, o * 128:(o + 1) * 128],
                              pT[0:RR, 0:128]),
                          [("pe", tv)], inc="dve")
            bank_done(BK_T, "dve", weff_done)
        if DEBUG:
            dma("sync", "outS", dbg["dbg_wef"], weff01T[:, :],
                [("dve", weff_done)])

        for i in range(8):
            dma("sync", "xldS",
                x_ar.ap().rearrange("p (c n) -> p c n", n=T)[
                    :, i * 4:(i + 1) * 4, :],
                xT.rearrange("(c p) n -> p c n", p=128)[
                    :, i * 4:(i + 1) * 4, :])
        XLD = sc.tot("xldS")

        def _finish():
            for i in range(8):
                dma("sync", "outS",
                    y_out.rearrange("(c p) n -> p c n", p=128)[
                        :, i * 4:(i + 1) * 4, :],
                    x_ar.ap().rearrange("p (c n) -> p c n", n=T)[
                        :, i * 4:(i + 1) * 4, :],
                    [("dve", sc.tot("dve")), ("act", sc.tot("act")),
                     ("pe", sc.tot("pe")), ("xldS", XLD)])
            E("gpsimd", lambda g: None,
              [(nm, sc.tot(nm)) for nm in semnames if sc.tot(nm) > 0])
            block = e(nc.Block())
            for eng_name in ENGINES:
                ops = sc.ops[eng_name]
                if not ops:
                    continue

                def body(engo, _ops=ops):
                    for (ws, fn, inc) in _ops:
                        for (sname, v) in ws:
                            engo.wait_ge(sems[sname], v)
                        instr = fn(engo)
                        if inc is not None and instr is not None:
                            instr.then_inc(sems[inc[0]], inc[1])

                getattr(block, eng_name)(body)

        if phases <= 0:
            _finish()
            return nc

        def ln_stats(src, col0, m_dst, r_dst, extra_waits):
            r1v, r2v = 0, 0
            for dc in range(DC):
                w = list(extra_waits(dc))
                cv = E("vector",
                       lambda v, ap=src(dc): v.tensor_copy(
                           xbscr[:, :], ap[:, col0:col0 + 512]),
                       w + [("pe", r1v)], inc="dve")
                sv = E("scalar",
                       lambda s, ap=src(dc): s.activation(
                           sqscr[:, :], ap[:, col0:col0 + 512], AF.Square),
                       w + [("pe", r2v)], inc="act")
                r1v = mm(BK_R1, pR1[:, :], onesb[:, :], xbscr[:, :],
                         start=(dc == 0), stop=(dc == DC - 1),
                         waits=[("dve", cv), ("dma_k", K_CONST)], inc=True)
                r2v = mm(BK_R2, pR2[:, :], onesb[:, :], sqscr[:, :],
                         start=(dc == 0), stop=(dc == DC - 1),
                         waits=[("act", sv)], inc=True)
            mv = E("vector",
                   lambda v: v.tensor_scalar(m_dst[:, :], pR1[:, :],
                                             1.0 / D, None, ALU.mult),
                   [("pe", r2v)], inc="dve")
            E("vector",
              lambda v: v.tensor_scalar(r_dst[:, :], pR2[:, :], 1.0 / D,
                                        EPS, ALU.mult, ALU.add),
              inc="dve")
            E("vector",
              lambda v: v.tensor_tensor(pR1[0:1, :], m_dst[:, :],
                                        m_dst[:, :], ALU.mult),
              inc="dve")
            vv = E("vector",
                   lambda v: v.tensor_tensor(r_dst[:, :], r_dst[:, :],
                                             pR1[0:1, :], ALU.subtract),
                   inc="dve")
            bank_done(BK_R1, "dve", vv)
            sv = E("scalar",
                   lambda s: s.activation(r_dst[:, :], r_dst[:, :], AF.Sqrt),
                   [("dve", vv)], inc="act")
            rv = E("vector", lambda v: v.reciprocal(r_dst[:, :], r_dst[:, :]),
                   [("act", sv)], inc="dve")
            bank_done(BK_R2, "dve", rv)
            return rv

        def bcast(row_ap, dst, waits):
            w = list(waits) + banks[BK_E].deps + [("dma_k", K_CONST)]
            banks[BK_E].deps = []
            pv = E("tensor",
                   lambda t: t.matmul(pE[:, :], onesrow[:, :], row_ap,
                                      start=True, stop=True),
                   w, inc="pe")
            cv = E("vector", lambda v: v.tensor_copy(dst[:, :], pE[:, :]),
                   [("pe", pv)], inc="dve")
            bank_done(BK_E, "dve", cv)
            return cv

        def e01_calc(ysrc, ywait):
            pv = 0
            for dc in range(DC):
                pv = mm(BK_E, pE[0:RR, :],
                        enc01T[:, dc * RR:(dc + 1) * RR], ysrc(dc),
                        start=(dc == 0), stop=(dc == DC - 1),
                        waits=[("dve", ywait), ("gkS", gk)])
            ev = E("vector",
                   lambda v: v.tensor_copy(e01_sb[:, :], pE[0:RR, :]),
                   [("pe", pv)], inc="dve")
            bank_done(BK_E, "dve", ev)
            return ev

        def ffn_half(src, src_col0, ydst, out_cb, stats_extra):
            pe_entry = sc.tot("pe")
            rv = ln_stats(src, src_col0, m_sb, rstd_sb, stats_extra)
            b0 = bcast(rstd_sb[:, :], bc0, [("dve", rv)])
            b1 = bcast(m_sb[:, :], bc1, [("dve", rv)])
            ylast = 0
            for dc in range(DC):
                E("vector",
                  lambda v, sa=src(dc): v.tensor_tensor(
                      yf32[:, :], sa[:, src_col0:src_col0 + 512], bc1[:, :],
                      ALU.subtract),
                  [("dve", b1), ("pe", pe_entry)], inc="dve")
                ylast = E("vector",
                          lambda v, yd=ydst(dc): v.tensor_tensor(
                              yd, yf32[:, :], bc0[:, :], ALU.mult),
                          [("dve", b0)], inc="dve")
            ev01 = e01_calc(ydst, ylast)
            for oc in range(DC):
                bi = oc % 4
                for hp in range(2):
                    pw = load_panel(hp, wb_p[2 * oc + hp], cast=True)
                    pvp = 0
                    for k in range(16):
                        dcc = hp * 16 + k
                        pvp = mm(bi, pmain[bi][:, :],
                                 wp[hp][:, k * 128:(k + 1) * 128], ydst(dcc),
                                 start=(dcc == 0), stop=False,
                                 waits=[pw, ("dve", ylast)], inc=(k == 15))
                    st["wp_free"][hp] = [("pe", pvp)]
                pv = mm(bi, pmain[bi][:, :],
                        weff01T[:, oc * 128:(oc + 1) * 128], e01_sb[:, :],
                        start=False, stop=True,
                        waits=[("dve", ev01), ("dve", weff_done)])
                out_cb(oc, pv, bi)

        def make_ffn1_out(half):
            def out_cb(oc, pv, bi):
                ti = oc % 2
                av = E("scalar",
                       lambda s, o=oc, b=bi, t=ti: s.activation(
                           t_ev[t][:, :], pmain[b][:, :], AF.Identity,
                           bias=b_half[:, o:o + 1], scale=0.5),
                       [("pe", pv), ("dve", c_bh)] + st["tev_free"][ti],
                       inc="act")
                bank_done(bi, "act", av)
                xv = E("vector",
                       lambda v, o=oc, t=ti, h=half: v.tensor_tensor(
                           x_t(o)[:, h * 512:(h + 1) * 512],
                           x_t(o)[:, h * 512:(h + 1) * 512],
                           t_ev[t][:, :], ALU.add),
                       [("act", av)], inc="dve")
                st["tev_free"][ti] = [("dve", xv)]
            return out_cb

        for half in range(2):
            ffn_half(x_t, half * 512,
                     lambda dc: y1h[:, dc * 512:(dc + 1) * 512],
                     make_ffn1_out(half),
                     stats_extra=lambda dc: [("xldS", XLD),
                                             ("dve", weff_done)])
        ffn1_done = sc.tot("dve")
        if DEBUG:
            for i in range(8):
                dma("sync", "outS",
                    dbg["dbg_x1"].rearrange("(c p) n -> p c n", p=128)[
                        :, i * 4:(i + 1) * 4, :],
                    x_ar.ap().rearrange("p (c n) -> p c n", n=T)[
                        :, i * 4:(i + 1) * 4, :],
                    [("dve", ffn1_done)])
        if phases <= 1:
            _finish()
            return nc

        rv0 = ln_stats(x_t, 0, m_sb, rstd_sb, lambda dc: [("dve", ffn1_done)])
        rv1 = ln_stats(x_t, 512, m2_sb, r2_sb, lambda dc: [])
        for dc in range(DC):
            dma("sync", "spillS", x_dram.ap()[:, dc * T:(dc + 1) * T],
                x_t(dc), [("dve", ffn1_done)])
        SPILL = sc.tot("spillS")
        bB0 = bcast(m_sb[:, :], bc1, [("dve", rv0)])
        bA0 = bcast(rstd_sb[:, :], bc0, [("dve", rv0)])
        bA1 = bcast(r2_sb[:, :], bcA2, [("dve", rv1)])
        bB1 = bcast(m2_sb[:, :], bcB2, [("dve", rv1)])
        y2_done = 0
        for dc in range(DC):
            for half, bR, bM, wv in ((0, bc0, bc1, bA0), (1, bcA2, bcB2, bB1)):
                E("vector",
                  lambda v, d=dc, h=half, m=bM: v.tensor_tensor(
                      yf32[:, :],
                      x_t(d)[:, h * 512:(h + 1) * 512], m[:, :],
                      ALU.subtract),
                  [("dve", wv), ("dve", bB1)], inc="dve")
                y2_done = E("vector",
                            lambda v, d=dc, h=half, r=bR: v.tensor_tensor(
                                y2_t(d)[:, h * 512:(h + 1) * 512],
                                yf32[:, :], r[:, :], ALU.mult),
                            [("spillS", SPILL)], inc="dve")
        if DEBUG:
            for dc in range(DC):
                dma("sync", "outS", dbg["dbg_y2"][dc * 128:(dc + 1) * 128, :],
                    y2_t(dc), [("dve", y2_done)])
        if phases <= 2:
            _finish()
            return nc

        qkv_store = [qkvA, qkvB]
        qs_free = [[("dve", y2_done)], [("dve", y2_done)]]
        dk_inst = [0, 0]
        dk_free = [[("dve", y2_done)], [("dve", y2_done)]]
        ah_free = [("dve", y2_done)]
        gn_free = [("dve", y2_done)]
        ag_free = [("dve", y2_done)]
        tr_gate = [("dve", y2_done), ("spillS", SPILL)]
        sq_init = True

        for h in range(ATTN_HEADS):
            si = h % 2
            store = qkv_store[si]
            dki = h % 2
            bd = float(BLOCK_DECAY[h])
            dk_inst[dki] += 1
            dma("scalar", f"dkS{dki}", dker[dki][:, :], dker_in[h],
                waits=dk_free[dki])
            cdk_w = (f"dkS{dki}", 16 * dk_inst[dki])
            qkv_done = 0
            for ot3 in range(3):
                ptile = 3 * h + ot3
                bi0, bi1 = (2 * ot3) % 4, (2 * ot3 + 1) % 4
                pv = 0
                for hp in range(2):
                    pw = load_panel(hp, qkv_p[2 * ptile + hp], cast=True)
                    for k in range(16):
                        dcc = hp * 16 + k
                        for half, bi in ((0, bi0), (1, bi1)):
                            pv = mm(bi, pmain[bi][:, :],
                                    wp[hp][:, k * 128:(k + 1) * 128],
                                    y2_t(dcc)[:, half * 512:(half + 1) * 512],
                                    start=(dcc == 0), stop=(dcc == DC - 1),
                                    waits=[pw, ("dve", y2_done)] +
                                          (qs_free[si] if (dcc == 0 and
                                           ot3 == 0 and half == 0) else []),
                                    inc=(k == 15 and half == 1))
                    st["wp_free"][hp] = [("pe", pv)]
                for half, bi in ((0, bi0), (1, bi1)):
                    qkv_done = E(
                        "scalar",
                        lambda s, b=bi,
                        ap=store[:, ot3 * T + half * 512:
                                 ot3 * T + (half + 1) * 512]:
                        s.activation(ap, pmain[b][:, :], AF.Silu),
                        [("pe", pv)], inc="act")
                    bank_done(bi, "act", qkv_done)
            for b in range(NB):
                for mc in range(2):
                    dma("sync", "trS", vT[:, b * 256 + mc * 128:
                                          b * 256 + (mc + 1) * 128],
                        store[:, 2 * T + b * 256 + mc * 128:
                              2 * T + b * 256 + (mc + 1) * 128],
                        tr_gate + [("act", qkv_done)], transpose=True)
            gpv = 0
            for k in range(16):
                pw = (load_panel(0, gate_p[2 * h], cast=True)
                      if k == 0 else pw)
                for half, bi in ((0, 0), (1, 1)):
                    gpv = mm(bi, pmain[bi][:, :],
                             wp[0][:, k * 128:(k + 1) * 128],
                             y2_t(k)[:, half * 512:(half + 1) * 512],
                             start=(k == 0), stop=False,
                             waits=[pw], inc=(k == 15 and half == 1))
            st["wp_free"][0] = [("pe", gpv)]
            cmm_hist = [0, 0]
            avm = 0
            for b in range(NB):
                kslot = kdecT if b % 2 == 0 else kdecT2
                for mc in range(2):
                    dma("sync", "trS", kslot[:, mc * 128:(mc + 1) * 128],
                        store[:, T + b * 256 + mc * 128:
                              T + b * 256 + (mc + 1) * 128],
                        tr_gate + [("act", qkv_done),
                                   ("pe", cmm_hist[b % 2])],
                        transpose=True)
                trk_s = sc.tot("trS")
                kv2 = 0
                for mc in range(2):
                    kv2 = E("vector",
                            lambda v, ks=kslot, m=mc, hh=h: v.tensor_scalar(
                                ks[:, m * 128:(m + 1) * 128],
                                ks[:, m * 128:(m + 1) * 128],
                                kdecst[:, m * HEADS + hh:m * HEADS + hh + 1],
                                None, ALU.mult),
                            [("trS", trk_s), ("dma_k", K_CONST)], inc="dve")
                pv = 0
                for mc in range(2):
                    pv = mm(BK_E, pE[:, 0:128],
                            kslot[:, mc * 128:(mc + 1) * 128],
                            vT[:, b * 256 + mc * 128:b * 256 + (mc + 1) * 128],
                            start=(mc == 0), stop=(mc == 1),
                            waits=[("dve", kv2)])
                cmm_hist[b % 2] = pv
                cvv = E("vector",
                        lambda v, b_=b: v.tensor_copy(
                            Cst[:, b_ * 128:(b_ + 1) * 128], pE[:, 0:128]),
                        [("pe", pv)], inc="dve")
                bank_done(BK_E, "dve", cvv)
                if b == 0:
                    avm = E("vector",
                            lambda v: v.tensor_copy(Ah[:, :], Cst[:, 0:128]),
                            ah_free, inc="dve")
                else:
                    E("vector",
                      lambda v, d=bd: v.tensor_scalar(Ah[:, :], Ah[:, :], d,
                                                      None, ALU.mult),
                      inc="dve")
                    avm = E("vector",
                            lambda v, b_=b: v.tensor_tensor(
                                Ah[:, :], Ah[:, :],
                                Cst[:, b_ * 128:(b_ + 1) * 128], ALU.add),
                            inc="dve")
            cs = dma("scalar", "csS", cc_in[h].ap(), Ah[:, :],
                     [("dve", avm)])
            ah_free = [("csS", cs)]
            if USE_CC:
                E("gpsimd",
                  lambda g, h_=h: g.collective_compute(
                      "AllGather", ALU.bypass,
                      replica_groups=[[0, 1, 2, 3], [4, 5, 6, 7]],
                      ins=[cc_in[h_].ap().opt()],
                      outs=[cc_out[h_].ap().opt()]),
                  [("csS", cs)], inc=f"cc{h}")
            else:
                for r in range(CPS):
                    E("gpsimd",
                      lambda g, h_=h, r_=r: g.dma_start(
                          out=cc_out[h_].ap()[r_ * 128:(r_ + 1) * 128, :],
                          in_=cc_in[h_].ap()),
                      [("csS", cs)], inc=f"cc{h}", inck=16)
            pv = 0
            pw = load_panel(1, gate_p[2 * h + 1], cast=True)
            for k in range(16):
                dcc = 16 + k
                for half, bi in ((0, 0), (1, 1)):
                    pv = mm(bi, pmain[bi][:, :],
                            wp[1][:, k * 128:(k + 1) * 128],
                            y2_t(dcc)[:, half * 512:(half + 1) * 512],
                            start=False, stop=(dcc == DC - 1),
                            waits=[pw], inc=(k == 15 and half == 1))
            st["wp_free"][1] = [("pe", pv)]
            gv = 0
            for half, bi in ((0, 0), (1, 1)):
                gv = E("scalar",
                       lambda s, b=bi, hf=half: s.activation(
                           gatenw[:, hf * 512:(hf + 1) * 512],
                           pmain[b][:, :], AF.Sigmoid),
                       [("pe", pv)] + gn_free, inc="act")
                bank_done(bi, "act", gv)
            gnv = E("vector",
                    lambda v, hh=h: v.tensor_scalar(
                        gatenw[:, :], gatenw[:, :], nw[:, hh:hh + 1],
                        None, ALU.mult),
                    [("act", gv), ("dma_k", K_CONST)], inc="dve")
            ag = dma("scalar", "agS", Agath[:, :],
                     cc_out[h].ap().rearrange("(r p) e -> p r e", p=128),
                     [(f"cc{h}", 1)] + ag_free)
            E("vector",
              lambda v, hh=h: v.tensor_scalar(
                  kvf[:, :], Agath[:, 0:128],
                  prefw[:, hh * CPS:hh * CPS + 1], None, ALU.mult),
              [("agS", ag), ("dma_k", K_CONST)], inc="dve")
            for r in range(1, CPS):
                E("vector",
                  lambda v, r_=r, hh=h: v.tensor_scalar(
                      sc128[:, :], Agath[:, r_ * 128:(r_ + 1) * 128],
                      prefw[:, hh * CPS + r_:hh * CPS + r_ + 1],
                      None, ALU.mult),
                  inc="dve")
                E("vector",
                  lambda v: v.tensor_tensor(kvf[:, :], kvf[:, :],
                                            sc128[:, :], ALU.add),
                  inc="dve")
            kbv = E("vector", lambda v: v.tensor_copy(kvb[:, :], kvf[:, :]),
                    inc="dve")
            last_attg = 0
            sq_free = []
            att2_free = []
            for b in range(NB):
                spv = []
                for mc in range(2):
                    spv.append(mm(
                        mc, pmain[mc][:, 0:256],
                        store[:, T + b * 256 + mc * 128:
                              T + b * 256 + (mc + 1) * 128],
                        store[:, b * 256:(b + 1) * 256],
                        start=True, stop=True, waits=[("act", qkv_done)]))
                sdv = []
                for mc in range(2):
                    dv = E("vector",
                           lambda v, m=mc, dk=dki: v.tensor_tensor(
                               sdecT[:, m * 256:(m + 1) * 256],
                               pmain[m][:, 0:256],
                               dker[dk][:, m * 256:(m + 1) * 256], ALU.mult),
                           [("pe", spv[mc]), cdk_w] + sq_free, inc="dve")
                    bank_done(mc, "dve", dv)
                    sdv.append(dv)
                qv = E("vector",
                       lambda v, ap=store[:, b * 256:(b + 1) * 256],
                       dk=dki: v.tensor_tensor(
                           qdecs[:, :], ap, dker[dk][:, 512:768], ALU.mult),
                       [("act", qkv_done), cdk_w] + sq_free, inc="dve")
                abi = 2 + (b % 2)
                for mc in range(2):
                    mm(abi, pmain[abi][:, 0:256],
                       vT[:, b * 256 + mc * 128:b * 256 + (mc + 1) * 128],
                       sdecT[:, mc * 256:(mc + 1) * 256],
                       start=(mc == 0), stop=False,
                       waits=[("dve", sdv[mc])])
                apv = mm(abi, pmain[abi][:, 0:256], kvb[:, :],
                         qdecs[:, :], start=False, stop=True,
                         waits=[("dve", qv), ("dve", kbv)])
                if b < NB - 1:
                    E("vector",
                      lambda v, d=bd: v.tensor_scalar(
                          kvf[:, :], kvf[:, :], d, None, ALU.mult),
                      [("pe", apv)], inc="dve")
                    E("vector",
                      lambda v, b_=b: v.tensor_tensor(
                          kvf[:, :], kvf[:, :],
                          Cst[:, b_ * 128:(b_ + 1) * 128], ALU.add),
                      inc="dve")
                    kbv = E("vector",
                            lambda v: v.tensor_copy(kvb[:, :], kvf[:, :]),
                            inc="dve")
                last_attg = E("vector",
                              lambda v, a0=pmain[abi][:, 0:256],
                              a1=gatenw[:, b * 256:(b + 1) * 256],
                              ao=attg_t(h)[:, b * 256:(b + 1) * 256]:
                              v.tensor_tensor(ao, a0, a1, ALU.mult),
                              [("dve", gnv), ("pe", apv)] +
                              ([("spillS", SPILL)] if (h == 0 and b == 0)
                               else []),
                              inc="dve")
                sqv = E("scalar",
                        lambda s, a=abi: s.activation(
                            att2[:, :], pmain[a][:, 0:256], AF.Square),
                        [("pe", apv), ("dve", last_attg)] + att2_free,
                        inc="act")
                bank_done(abi, "dve", last_attg)
                bank_add(abi, "act", sqv)
                w = banks[BK_R1].deps
                banks[BK_R1].deps = []
                spq = E("tensor",
                        lambda t: t.matmul(pR1[0:1, 0:256], onesb[:, :],
                                           att2[:, :], start=True,
                                           stop=True),
                        [("act", sqv)] + w, inc="pe")
                if sq_init:
                    sqa = E("vector",
                            lambda v, b_=b: v.tensor_copy(
                                sumsq_row[:, b_ * 256:(b_ + 1) * 256],
                                pR1[0:1, 0:256]),
                            [("pe", spq)], inc="dve")
                else:
                    sqa = E("vector",
                            lambda v, b_=b: v.tensor_tensor(
                                sumsq_row[:, b_ * 256:(b_ + 1) * 256],
                                sumsq_row[:, b_ * 256:(b_ + 1) * 256],
                                pR1[0:1, 0:256], ALU.add),
                            [("pe", spq)], inc="dve")
                bank_done(BK_R1, "dve", sqa)
                sq_free = [("pe", apv)]
                att2_free = [("pe", spq)]
            sq_init = False
            qs_free[si] = [("dve", sc.tot("dve"))]
            dk_free[dki] = [("dve", sc.tot("dve"))]
            gn_free = [("dve", last_attg)]
            ag_free = [("dve", kbv)]
            tr_gate = [("pe", sc.tot("pe"))]
        attn_done = sc.tot("dve")
        if DEBUG:
            dma("sync", "outS", dbg["dbg_ali"], xali[:, :],
                [("dve", attn_done)])
        if phases <= 3:
            _finish()
            return nc

        tsv = E("vector",
                lambda v: v.tensor_scalar(rstd_row[:, :], sumsq_row[:, :],
                                          1.0 / D, EPS, ALU.mult, ALU.add),
                [("dve", attn_done)], inc="dve")
        ssv = E("scalar",
                lambda s: s.activation(rstd_row[:, :], rstd_row[:, :],
                                       AF.Sqrt),
                [("dve", tsv)], inc="act")
        rrv = E("vector",
                lambda v: v.reciprocal(rstd_row[:, :], rstd_row[:, :]),
                [("act", ssv)], inc="dve")
        if DEBUG:
            dma("sync", "outS", dbg["dbg_rstd"], rstd_row[:, :],
                [("dve", rrv)])
        bb0 = bcast(rstd_row[:, 0:512], bc0, [("dve", rrv)])
        bb1 = bcast(rstd_row[:, 512:1024], bc1, [("dve", rrv)])
        for oc in range(DC):
            bi0, bi1 = (2 * oc) % 4, (2 * oc + 1) % 4
            pv = 0
            for hp in range(2):
                pw = load_panel(hp, out_p[2 * oc + hp], cast=True)
                for k in range(16):
                    hd = hp * 16 + k
                    for half, bi in ((0, bi0), (1, bi1)):
                        pv = mm(bi, pmain[bi][:, :],
                                wp[hp][:, k * 128:(k + 1) * 128],
                                attg_t(hd)[:, half * 512:(half + 1) * 512],
                                start=(hd == 0), stop=(hd == DC - 1),
                                waits=[pw, ("dve", attn_done)],
                                inc=(k == 15 and half == 1))
                st["wp_free"][hp] = [("pe", pv)]
            for half, bi, bcr, bv in ((0, bi0, bc0, bb0), (1, bi1, bc1, bb1)):
                ti = (2 * oc + half) % 2
                tv = E("vector",
                       lambda v, b=bi, t=ti, r=bcr: v.tensor_tensor(
                           t_ev[t][:, :], pmain[b][:, :], r[:, :], ALU.mult),
                       [("pe", pv), ("dve", bv)] + st["tev_free"][ti],
                       inc="dve")
                bank_done(bi, "dve", tv)
                st["acc_inst"][ti] += 1
                dma("sync", f"accS{ti}",
                    x2_dram.ap()[:, oc * T + half * 512:
                                 oc * T + (half + 1) * 512],
                    t_ev[ti][:, :], [("dve", tv)])
                st["tev_free"][ti] = [(f"accS{ti}", 16 * st["acc_inst"][ti])]
        ACC0 = sc.tot("accS0")
        ACC1 = sc.tot("accS1")
        if phases <= 4:
            _finish()
            return nc

        for half in range(2):
            for i in range(4):
                dma("sync", "xhS",
                    xh.ap().rearrange("p (c n) -> p c n", n=512)[
                        :, i * 8:(i + 1) * 8, :],
                    x_dram.ap().rearrange("p (c n) -> p c n", n=T)[
                        :, i * 8:(i + 1) * 8,
                        half * 512:(half + 1) * 512],
                    [("accS0", ACC0), ("accS1", ACC1), ("spillS", SPILL),
                     ("pe", sc.tot("pe")), ("dve", sc.tot("dve"))])
            XH = sc.tot("xhS")
            for i in range(4):
                dma("gpsimd", "accD",
                    xh.ap().rearrange("p (c n) -> p c n", n=512)[
                        :, i * 8:(i + 1) * 8, :],
                    x2_dram.ap().rearrange("p (c n) -> p c n", n=T)[
                        :, i * 8:(i + 1) * 8,
                        half * 512:(half + 1) * 512],
                    [("xhS", XH), ("accS0", ACC0), ("accS1", ACC1)],
                    accum_op=ALU.add)
            XHA = sc.tot("accD")

            def xh_t(dc):
                return xh[:, dc * 512:(dc + 1) * 512]

            def ffn2_out(oc, pv, bi, h=half):
                ti = oc % 2
                av = E("scalar",
                       lambda s, o=oc, b=bi, t=ti: s.activation(
                           t_ev[t][:, :], pmain[b][:, :], AF.Identity,
                           bias=b_half[:, o:o + 1], scale=0.5),
                       [("pe", pv), ("dve", c_bh)] + st["tev_free"][ti],
                       inc="act")
                bank_done(bi, "act", av)
                xv = E("vector",
                       lambda v, o=oc, t=ti: v.tensor_tensor(
                           xh_t(o), xh_t(o), t_ev[t][:, :], ALU.add),
                       [("act", av)], inc="dve")
                st["tev_free"][ti] = [("dve", xv)]
                dma("sync", "outS",
                    y_out[128 * oc:128 * (oc + 1), h * 512:(h + 1) * 512],
                    xh_t(oc), [("dve", xv)])

            ffn_half(xh_t, 0,
                     lambda dc: y3h[:, dc * 512:(dc + 1) * 512],
                     ffn2_out,
                     stats_extra=lambda dc, x_=XHA: [("accD", x_)])

        E("gpsimd", lambda g: None,
          [(nm, sc.tot(nm)) for nm in semnames if sc.tot(nm) > 0])

        block = e(nc.Block())
        for eng_name in ENGINES:
            ops = sc.ops[eng_name]
            if not ops:
                continue

            def body(engo, _ops=ops):
                for (ws, fn, inc) in _ops:
                    for (sname, v) in ws:
                        engo.wait_ge(sems[sname], v)
                    instr = fn(engo)
                    if inc is not None and instr is not None:
                        instr.then_inc(sems[inc[0]], inc[1])

            getattr(block, eng_name)(body)

    return nc


_CACHED = {}


def _host_inputs(inputs):
    qkv_w = np.asarray(inputs["L0_attn_qkv_w"], dtype=np.float32)
    out_w = np.asarray(inputs["L0_attn_out_w"], dtype=np.float32)
    gate_w = np.asarray(inputs["L0_attn_gate_w"], dtype=np.float32)
    norm_w = np.asarray(inputs["L0_attn_norm_w"], dtype=np.float32)
    w_base = np.asarray(inputs["L0_ffn_W_base"], dtype=np.float32)
    b_base = np.asarray(inputs["L0_ffn_b_base"], dtype=np.float32)
    enc0 = np.asarray(inputs["L0_ffn_enc0"], dtype=np.float32)
    enc1 = np.asarray(inputs["L0_ffn_enc1"], dtype=np.float32)
    coefs0 = np.asarray(inputs["L0_ffn_coefs0"], dtype=np.float32)
    coefs1 = np.asarray(inputs["L0_ffn_coefs1"], dtype=np.float32)
    mask0 = np.asarray(inputs["L0_ffn_mask0"])
    mask1 = np.asarray(inputs["L0_ffn_mask1"])

    c01 = np.concatenate([coefs0, coefs1], axis=2)
    m01 = np.concatenate([mask0, mask1], axis=2)
    shared = {
        "qkv_p": _panelize(qkv_w, 3 * HEADS * HD, D),
        "gate_p": _panelize(gate_w, D, D),
        "out_p": _panelize(out_w, D, D),
        "wb_p": _panelize(w_base, D, D),
        "mt_p": _panelize(_fourier_m().astype(np.float32), D, D).astype(
            ml_dtypes.bfloat16),
        "enc01t": np.ascontiguousarray(
            np.concatenate([enc0, enc1], axis=0).T),
        "wcoef": np.ascontiguousarray(
            c01.transpose(1, 0, 2).reshape(D, NE * RR)),
        "wmask": np.ascontiguousarray(
            m01.transpose(1, 0, 2).reshape(D, NE * RR).astype(np.uint8)),
        "bt_in": np.ascontiguousarray(b_base.reshape(DC, 128).T),
        "nw_in": np.ascontiguousarray(norm_w.reshape(DC, 128).T),
        "kdec_in": _kdecay_table(),
        "ident_in": np.eye(128, dtype=np.float32).astype(ml_dtypes.bfloat16),
        "ones_in": np.ones((128, 2), dtype=np.float32),
        "onesb_in": np.ones((128, 1), dtype=np.float32).astype(
            ml_dtypes.bfloat16),
        "onesrow_in": np.ones((1, 128), dtype=np.float32),
        "dker_in": _decay_tables(),
    }
    x = np.asarray(inputs["L0_attn_hidden_states"], dtype=np.float32)
    xr = x.reshape(B, CPS, S // CPS, D)
    in_maps = []
    for c in range(NCORE):
        m = dict(shared)
        m["xT"] = np.ascontiguousarray(xr[c // CPS, c % CPS].T)
        m["prefw_in"] = _prefix_weights(c)
        in_maps.append(m)
    return in_maps


def kernel(**inputs):
    in_maps = _host_inputs(inputs)
    if "nc" not in _CACHED:
        _CACHED["nc"] = build_program()
    res = run_bass_kernel_spmd(_CACHED["nc"], in_maps,
                               core_ids=list(range(NCORE)))
    out = np.empty((B, CPS, S // CPS, D), dtype=np.float32)
    for c in range(NCORE):
        out[c // CPS, c % CPS] = res.results[c]["y_out"].T
    return out.reshape(B, S, D)


if __name__ == "__main__":
    nc = build_program()
    print("build ok", sum(1 for _ in nc.all_instructions()))
